# version 01
# baseline (speedup 1.0000x reference)
import math
import numpy as np
import ml_dtypes
import concourse.bass as bass
import concourse.mybir as mybir
from concourse.bass_utils import run_bass_kernel_spmd

F32 = mybir.dt.float32
BF16 = mybir.dt.bfloat16
AF = mybir.ActivationFunctionType
ALU = mybir.AluOpType
AX = mybir.AxisListType

T = 2048
D = 1024
NH = 16
FF = 4096
NE = 8
EPS = 1e-5

ENGINES = ("tensor", "vector", "scalar", "gpsimd", "sync")


def _bucket_np(n):
    n = np.asarray(n)
    nf = np.maximum(n, 1).astype(np.float32) / np.float32(16.0)
    val = 16 + (np.log(nf) / np.float32(math.log(8.0)) * np.float32(16.0)).astype(np.int32)
    val = np.minimum(val, 31)
    return np.where(n < 16, n, val).astype(np.int64)


def _gather_bias_tiles(rel_emb):
    a = np.arange(128)
    qp, jj = a[:, None], a[None, :]
    idx = np.stack([
        _bucket_np(np.maximum(qp - jj, 0)),
        _bucket_np(qp - jj + 128),
        _bucket_np(np.maximum(jj - qp, 0)),
        _bucket_np(jj - qp + 128),
    ])
    g = np.asarray(rel_emb)[idx]
    return np.ascontiguousarray(g.transpose(3, 0, 1, 2)).astype(np.float32)


class Plan:
    def __init__(self, nc, sems):
        self.nc = nc
        self.sems = sems
        self.cnt = {k: 0 for k in sems}
        self.ops = {e: [] for e in ENGINES}
        self.order = 0

    def at(self, sem):
        return (sem, self.cnt[sem])

    ESEM = {"vector": "ve", "scalar": "se", "gpsimd": "ge"}

    def op(self, engine, fn, waits=(), inc=None, dma=False):
        waits = list(waits)
        if engine in self.ESEM and not dma:
            es = self.ESEM[engine]
            waits.append((es, self.cnt[es]))
            if inc is None:
                inc = es
        for s, v in waits:
            assert v <= self.cnt[s], f"wait on future {s}:{v} > {self.cnt[s]}"
        incs = None
        if inc is not None:
            amt = 16 if dma else 1
            self.cnt[inc] += amt
            incs = (inc, amt)
        import inspect as _insp
        ln = _insp.currentframe().f_back.f_lineno
        self.meta = getattr(self, "meta", [])
        self.meta.append((self.order, engine, ln))
        self.ops[engine].append((fn, tuple(waits), incs, self.order))
        self.order += 1

    def emit(self, block, limit=None):
        sems, ops = self.sems, self.ops

        def runner(name):
            def _run(eng):
                last = {}
                for fn, waits, incs, order in ops[name]:
                    if limit is not None and order >= limit:
                        continue
                    for s, v in waits:
                        if v > 0 and last.get(s, -1) < v:
                            eng.wait_ge(sems[s], v)
                            last[s] = v
                    instr = fn(eng)
                    if incs is not None:
                        instr.then_inc(sems[incs[0]], incs[1])
            return _run

        block.tensor(runner("tensor"))
        block.vector(runner("vector"))
        block.scalar(runner("scalar"))
        block.gpsimd(runner("gpsimd"))
        block.sync(runner("sync"))


def build_nc(debug=False, limit=None):
    from contextlib import ExitStack
    nc = bass.Bass("TRN2", target_bir_lowering=False, detect_race_conditions=False)

    def din(name, shape, dtype=F32):
        return nc.dram_tensor(name, shape, dtype, kind="ExternalInput").ap()

    X = din("X", [T, D])
    WQ = din("WQ", [D, D]); WK = din("WK", [D, D])
    WV = din("WV", [D, D]); WO = din("WO", [D, D])
    BQP = din("BQP", [128, 8]); BKP = din("BKP", [128, 8])
    BVROW = din("BVROW", [1, D]); BOROW = din("BOROW", [1, D])
    GB = din("GB", [NH, 4, 128, 128])
    REL31 = din("REL31", [1, NH])
    WGP = din("WGP", [128, 8, NE])
    W1TI = din("W1TI", [NE, 32, 8, 128, 128])
    B1P = din("B1P", [128, NE, 32])
    W2I = din("W2I", [NE, FF, D])
    B2I = din("B2I", [NE, D])
    CP = din("CP", [128, 8])
    WMOD = din("WMOD", [D, 6 * D])
    Y = nc.dram_tensor("Y", [T, D], F32, kind="ExternalOutput").ap()
    ACC = nc.dram_tensor("ACC", [T, D], F32, kind="Internal").ap()

    dbg = {}
    if debug:
        for name, shape, dt in [
            ("dMOD", [1, 6 * D], F32), ("dHT", [128, 8, T], BF16),
            ("dQT", [128, 8, T], BF16), ("dKT", [128, 8, T], BF16),
            ("dV", [128, 16, D], BF16), ("dMALL", [128, NH, 16], F32),
            ("dOT", [128, 8, T], BF16), ("dXA", [128, 16, D], F32),
            ("dH2T", [128, 8, T], BF16), ("dCOEF", [128, 16, NE], F32),
            ("dHS", [128, 32, 512], BF16), ("dGC", [128, 2, 4, 128], BF16),
            ("dGI", [128, 40], F32), ("dWG", [128, 8, NE], BF16), ("dB1", [128, NE, 32], F32),
        ]:
            dbg[name] = nc.dram_tensor(name, shape, dt, kind="ExternalOutput").ap()

    with ExitStack() as stack:
        ARENA = 212480
        backing = stack.enter_context(
            nc.sbuf_tensor("arena", [128, ARENA], mybir.dt.uint8))
        base_addr = nc.lookup_mloc(backing).addr

        def place(name, shape, dtype, off):
            per = 1
            for s in shape[1:]:
                per *= s
            per *= mybir.dt.size(dtype)
            assert off + per <= ARENA, f"{name} overflows arena: {off}+{per}"
            h = nc.virtual_sbuf_tensor(name, shape, dtype)
            nc._tensor(h.name, h.shape, h.dtype, type="SB")
            mloc = nc.lookup_mloc(h)
            mloc.allocated = True
            mloc.addr = base_addr + off
            return h

        x_sb = place("x_sb", [128, 16, D], F32, 0)
        stage = place("stage", [128, 4, D], BF16, 0)
        qT = place("qT", [128, 8, T], BF16, 0)
        kT = place("kT", [128, 8, T], BF16, 32768)
        hslab = place("hslab", [128, 32, 512], BF16, 0)
        fc_x = place("fc_x", [128, 2, D], F32, 32768)
        fc_a = place("fc_a", [128, 2, D], F32, 40960)
        fc_o = place("fc_o", [128, 2, D], F32, 49152)
        mod_row = place("mod_row", [1, 6 * D], F32, 65536)
        ca1_bc = place("ca1_bc", [128, D], F32, 90112)
        sa_bc = place("sa_bc", [128, D], F32, 94208)
        v_sb = place("v_sb", [128, 16, D], BF16, 65536)
        stage_v = place("stage_v", [128, 4, D], BF16, 65536)
        h2T = place("h2T", [128, 8, T], BF16, 65536)
        hT = place("hT", [128, 8, T], BF16, 98304)
        oT = place("oT", [128, 8, T], BF16, 98304)
        w2sb = place("w2sb", [128, 32, D], BF16, 98304)
        wqT = place("wqT", [128, 8, D], BF16, 131072)
        wvT = place("wvT", [128, 8, D], BF16, 131072)
        bo_f = place("bo_f", [1, D], F32, 131072)
        wkT = place("wkT", [128, 8, D], BF16, 147456)
        wmod_buf = place("wmod_buf", [128, 2, T], BF16, 147456)
        ptile = place("ptile", [128, 4, 512], BF16, 131072)
        graw = place("graw", [128, 2, 4, 128], F32, 135168)
        gcomb = place("gcomb", [128, 2, 4, 128], BF16, 139264)
        m_all = place("m_all", [128, 16], F32, 141312)
        mT16 = place("mT16", [16, 128], BF16, 141440)
        m_row = place("m_row", [1, 2, T], BF16, 141824)
        r_bf = place("r_bf", [128, 2, 512], BF16, 150016)
        rbc_sb = place("rbc_sb", [128, 512], F32, 152064)
        pexp_scr = place("pexp_scr", [128, 1024], F32, 154112)
        cb1_bc = place("cb1_bc", [128, D], F32, 154112)
        sb_bc = place("sb_bc", [128, D], F32, 158208)
        woT = place("woT", [128, 8, D], BF16, 163840)
        w1buf = place("w1buf", [128, 4, 8, 128], BF16, 163840)
        b1pp = place("b1pp", [128, NE, 32], F32, 172032)
        b2row = place("b2row", [1, D], BF16, 173056)
        ffslot = place("ffslot", [128, 2, 512], F32, 175104)
        iden_bf = place("iden_bf", [128, 128], BF16, 180224)
        iden_f = place("iden_f", [128, 128], F32, 180480)
        mask1 = place("mask1", [128, 128], BF16, 180992)
        mask1T = place("mask1T", [128, 128], BF16, 181248)
        ones_row_bf = place("ones_row_bf", [128, 128], BF16, 181504)
        ones_row_f = place("ones_row_f", [1, 128], F32, 181760)
        ones_col_bf = place("ones_col_bf", [128, 1], BF16, 182272)
        iota_s = place("iota_s", [128, 128], F32, 182336)
        scr = place("scr", [128, D], F32, 182848)
        h_tile = place("h_tile", [128, D], BF16, 186944)
        stats6 = place("stats6", [128, 12], F32, 188992)
        mv2 = place("mv2", [128, 2], F32, 189056)
        rstd = place("rstd", [128, 1], F32, 189120)
        rr = place("rr", [128, 1], F32, 189124)
        g_eq1 = place("g_eq1", [128, NE], F32, 189184)
        g_eq2 = place("g_eq2", [128, NE], F32, 189216)
        g_msk = place("g_msk", [128, NE], F32, 189248)
        g_m1n = place("g_m1n", [128, 1], F32, 189280)
        g_m2n = place("g_m2n", [128, 1], F32, 189284)
        g_d = place("g_d", [128, 1], F32, 189288)
        g_s = place("g_s", [128, 1], F32, 189292)
        g_r = place("g_r", [128, 1], F32, 189296)
        g_w2 = place("g_w2", [128, 1], F32, 189300)
        coef = place("coef", [128, 16, NE], F32, 189312)
        gb_bc = place("gb_bc", [128, D], F32, 189824)
        cb_row = place("cb_row", [1, D], F32, 193920)
        sb_row = place("sb_row", [1, D], F32, 198016)
        bv_row = place("bv_row", [1, D], BF16, 202112)
        gabo_row = place("gabo_row", [1, D], BF16, 204160)
        bq_pp = place("bq_pp", [128, 8], F32, 206208)
        bk_pp = place("bk_pp", [128, 8], F32, 206240)
        nr31 = place("nr31", [128, NH], F32, 206272)
        rel31row = place("rel31row", [1, NH], F32, 206336)
        c_col = place("c_col", [128, 8], BF16, 206400)
        c_sil = place("c_sil", [128, 8], BF16, 206416)
        wg_sb = place("wg_sb", [128, 8, NE], BF16, 206432)
        mod_pp = place("mod_pp", [128, 48], F32, 206560)
        zero_col = place("zero_col", [128, 1], F32, 206756)
        eps_col = place("eps_col", [128, 1], F32, 206752)

        ps_a = stack.enter_context(nc.psum_tensor("ps_a", [128, 2048], F32))
        ps_b = stack.enter_context(nc.psum_tensor("ps_b", [128, 1024], F32))
        ps_c = stack.enter_context(nc.psum_tensor("ps_c", [128, 512], F32))
        ps_t = stack.enter_context(nc.psum_tensor("ps_t", [128, 1024], BF16))

        semnames = ["te", "ve", "se", "ge", "gx", "gsm", "gw0", "gw1", "gst",
                    "gg0", "gg1", "gm0", "gm1", "gwa", "gwb", "gwc", "gwd", "gsc",
                    "gw2", "gb2", "ga0", "ga1", "gfl", "gfs0", "gfs1", "gxs",
                    "gdb"]
        sems = {n: stack.enter_context(nc.semaphore("s_" + n))
                for n in semnames}
        block = stack.enter_context(nc.Block())
        p = Plan(nc, sems)

        p.op("sync", lambda e: e.dma_start(
            out=x_sb[:, :, :], in_=X.rearrange("(c p) d -> p c d", p=128)),
            inc="gx", dma=True)
        x_loaded = p.at("gx")
        p.op("sync", lambda e: e.dma_start(out=bq_pp[:, :], in_=BQP[:, :]), inc="gsm", dma=True)
        p.op("sync", lambda e: e.dma_start(out=bk_pp[:, :], in_=BKP[:, :]), inc="gsm", dma=True)
        p.op("sync", lambda e: e.dma_start(out=rel31row[:, :], in_=REL31[:, :]), inc="gsm", dma=True)
        p.op("sync", lambda e: e.dma_start(out=bo_f[:, :], in_=BOROW[:, :]), inc="gsm", dma=True)

        p.op("gpsimd", lambda e: e.dma_start(out=c_col[:, :], in_=CP[:, :]), inc="gsc", dma=True)
        p.op("gpsimd", lambda e: e.dma_start(out=wg_sb[:, :, :], in_=WGP[:, :, :]), inc="gsc", dma=True)
        p.op("gpsimd", lambda e: e.dma_start(out=bv_row[:, :], in_=BVROW[:, :]), inc="gsc", dma=True)
        small_ld = p.at("gsm")
        cast_small = p.at("gsc")


        p.op("vector", lambda e: e.memset(ones_row_bf[:, :], 1.0), inc="ve")
        p.op("vector", lambda e: e.memset(ones_row_f[:, :], 1.0), inc="ve")
        p.op("vector", lambda e: e.memset(ones_col_bf[:, :], 1.0), inc="ve")
        p.op("vector", lambda e: e.memset(eps_col[:, :], EPS), inc="ve")
        p.op("vector", lambda e: e.memset(zero_col[:, :], 0.0), inc="ve")
        p.op("vector", lambda e: e.memset(ps_a[:, :], 0.0), inc="ve")
        p.op("vector", lambda e: e.memset(ps_b[:, :], 0.0), inc="ve")
        p.op("vector", lambda e: e.memset(ps_c[:, :], 0.0), inc="ve")
        psz = p.at("ve")
        p.op("gpsimd", lambda e: e.iota(iota_s[:, :], [[0, 1], [0, 1], [0, 1], [-1, 128]],
                                        base=0, channel_multiplier=1,
                                        allow_small_or_imprecise_dtypes=True), inc="ge")
        io1 = p.at("ge")
        p.op("vector", lambda e: e.tensor_scalar(iden_bf[:, :], iota_s[:, :], 0.0, None,
                                                 ALU.is_equal), waits=[io1], inc="ve")
        p.op("vector", lambda e: e.tensor_scalar(iden_f[:, :], iota_s[:, :], 0.0, None,
                                                 ALU.is_equal), waits=[io1], inc="ve")
        p.op("vector", lambda e: e.tensor_scalar(mask1[:, :], iota_s[:, :], 0.0, -30000.0,
                                                 ALU.is_lt, ALU.mult), waits=[io1], inc="ve")
        p.op("vector", lambda e: e.tensor_scalar(mask1T[:, :], iota_s[:, :], 0.0, -30000.0,
                                                 ALU.is_gt, ALU.mult), waits=[io1], inc="ve")
        consts_ready = p.at("ve")

        p.op("scalar", lambda e: e.activation(c_sil[:, :], c_col[:, :], AF.Silu),
             waits=[cast_small], inc="se")
        csil_ready = p.at("se")

        for ntb in range(3):
            for kt in range(8):
                slot = kt % 2
                w_wait = [p.at("te")] if (ntb or kt >= 2) else []
                p.op("gpsimd", lambda e, ntb=ntb, kt=kt, slot=slot: e.dma_start(
                    out=wmod_buf[:, slot, :],
                    in_=WMOD[128 * kt:128 * kt + 128, 2048 * ntb:2048 * ntb + 2048]),
                    waits=w_wait, inc="gw0" if slot == 0 else "gw1", dma=True)
                ld = p.at("gw0" if slot == 0 else "gw1")
                for nt in range(4):
                    p.op("tensor", lambda e, kt=kt, slot=slot, nt=nt: e.matmul(
                        ps_a[0:1, 512 * nt:512 * nt + 512],
                        c_sil[:, kt:kt + 1], wmod_buf[:, slot, 512 * nt:512 * nt + 512],
                        start=(kt == 0), stop=(kt == 7)),
                        waits=[ld, csil_ready, psz] + ([p.at("ve")] if (ntb and kt == 0 and nt == 0) else []),
                        inc="te" if nt == 3 else None)
            mm_done = p.at("te")
            p.op("vector", lambda e, ntb=ntb: e.tensor_copy(
                mod_row[0:1, 2048 * ntb:2048 * ntb + 2048], ps_a[0:1, 0:2048]),
                waits=[mm_done], inc="ve")
        mod_raw = p.at("ve")

        for off in (2 * D, 5 * D):
            p.op("vector", lambda e, off=off: e.tensor_scalar(
                scr[0:1, 0:D], mod_row[0:1, off:off + D], 0.0, 2.0, ALU.is_ge, ALU.mult),
                waits=[mod_raw], inc="ve")
            p.op("vector", lambda e: e.tensor_scalar(
                scr[0:1, 0:D], scr[0:1, 0:D], -1.0, None, ALU.add), inc="ve")
            p.op("vector", lambda e, off=off: e.tensor_scalar(
                ca1_bc[0:1, 0:D], mod_row[0:1, off:off + D], -1.0, None, ALU.mult),
                inc="ve")
            p.op("vector", lambda e, off=off: e.tensor_tensor(
                mod_row[0:1, off:off + D], mod_row[0:1, off:off + D], ca1_bc[0:1, 0:D],
                ALU.max), inc="ve")
            p.op("vector", lambda e, off=off: e.tensor_scalar(
                mod_row[0:1, off:off + D], mod_row[0:1, off:off + D], 0.25, None,
                ALU.max), inc="ve")
            p.op("vector", lambda e, off=off: e.tensor_tensor(
                mod_row[0:1, off:off + D], mod_row[0:1, off:off + D], scr[0:1, 0:D],
                ALU.mult), inc="ve")
        mod_ready = p.at("ve")
        if debug:
            p.op("sync", lambda e: e.dma_start(out=dbg["dMOD"][:, :], in_=mod_row[:, :]),
                 waits=[mod_ready], inc="gdb", dma=True)

        for c6 in range(48):
            p.op("tensor", lambda e, c6=c6: e.transpose(
                ps_c[:, c6:c6 + 1], mod_row[0:1, 128 * c6:128 * c6 + 128],
                iden_f[0:1, 0:1]),
                waits=[mod_ready, consts_ready], inc="te" if c6 == 47 else None)
        p.op("vector", lambda e: e.tensor_copy(mod_pp[:, :], ps_c[:, 0:48]),
             waits=[p.at("te")], inc="ve")
        modpp_ready = p.at("ve")

        p.op("tensor", lambda e: e.matmul(ps_b[:, 0:NH], ones_row_f[:, :], rel31row[:, :],
                                          start=True, stop=True),
             waits=[small_ld, consts_ready, modpp_ready], inc="te")
        p.op("vector", lambda e: e.tensor_scalar(nr31[:, :], ps_b[:, 0:NH], -0.125, None,
                                                 ALU.mult), waits=[p.at("te")], inc="ve")
        nr31_ready = p.at("ve")

        def bc_build(dst, src_off, add1, wait_extra):
            for c in range(2):
                p.op("tensor", lambda e, src_off=src_off, c=c: e.matmul(
                    ps_b[:, 512 * c:512 * c + 512], ones_row_f[:, :],
                    mod_row[0:1, src_off + 512 * c:src_off + 512 * c + 512],
                    start=True, stop=True),
                    waits=[mod_ready] + wait_extra, inc="te" if c == 1 else None)
            mmd = p.at("te")
            for c in range(2):
                if add1:
                    p.op("vector", lambda e, dst=dst, c=c: e.tensor_scalar(
                        dst[:, 512 * c:512 * c + 512], ps_b[:, 512 * c:512 * c + 512],
                        1.0, None, ALU.add), waits=[mmd], inc="ve")
                else:
                    p.op("vector", lambda e, dst=dst, c=c: e.tensor_copy(
                        dst[:, 512 * c:512 * c + 512], ps_b[:, 512 * c:512 * c + 512]),
                        waits=[mmd], inc="ve")

        bc_build(ca1_bc, 1 * D, True, [nr31_ready])
        bc_build(sa_bc, 0 * D, False, [p.at("ve")])
        bc_build(gb_bc, 5 * D, False, [p.at("ve")])
        p.op("vector", lambda e: e.tensor_copy(cb_row[:, :], mod_row[0:1, 4 * D:5 * D]),
             waits=[mod_ready], inc="ve")
        p.op("vector", lambda e: e.tensor_copy(sb_row[:, :], mod_row[0:1, 3 * D:4 * D]),
             waits=[mod_ready], inc="ve")
        p.op("vector", lambda e: e.tensor_tensor(
            gabo_row[:, :], bo_f[0:1, :], mod_row[0:1, 2 * D:3 * D], ALU.mult),
            waits=[mod_ready, small_ld], inc="ve")
        bc_ready = p.at("ve")

        def ln_block(dst_T, c1bc, sbc, waits0):
            for tt in range(16):
                w = list(waits0) if tt == 0 else [p.at("te")]
                p.op("vector", lambda e, tt=tt: e.bn_stats(
                    stats6[:, 0:6], x_sb[:, tt, 0:512]), waits=w + [x_loaded], inc="ve")
                p.op("vector", lambda e, tt=tt: e.bn_stats(
                    stats6[:, 6:12], x_sb[:, tt, 512:1024]), inc="ve")
                p.op("vector", lambda e: e.bn_aggr(mv2[:, :], stats6[:, :]),
                     waits=[p.at("ve")], inc="ve")
                sd = p.at("ve")
                p.op("scalar", lambda e: e.activation(rstd[:, :], mv2[:, 1:2], AF.Sqrt,
                                                      bias=eps_col[:, 0:1]), waits=[sd], inc="se")
                p.op("vector", lambda e: e.reciprocal(rr[:, :], rstd[:, :]),
                     waits=[p.at("se")], inc="ve")
                p.op("vector", lambda e, tt=tt: e.tensor_scalar(
                    scr[:, :], x_sb[:, tt, :], mv2[:, 0:1], rr[:, 0:1],
                    ALU.subtract, ALU.mult), waits=[p.at("ve")], inc="ve")
                p.op("vector", lambda e, c1bc=c1bc: e.tensor_tensor(
                    scr[:, :], scr[:, :], c1bc[:, :], ALU.mult), inc="ve")
                p.op("vector", lambda e, sbc=sbc: e.tensor_tensor(
                    h_tile[:, :], scr[:, :], sbc[:, :], ALU.add), inc="ve")
                hready = p.at("ve")
                for dt in range(8):
                    p.op("tensor", lambda e, dt=dt: e.transpose(
                        ps_t[:, 128 * dt:128 * dt + 128],
                        h_tile[:, 128 * dt:128 * dt + 128], iden_bf[:, :]),
                        waits=[hready], inc="te" if dt == 7 else None)
                p.op("vector", lambda e, tt=tt, dst_T=dst_T: e.tensor_copy(
                    dst_T[:, :, 128 * tt:128 * tt + 128], ps_t[:, :]),
                    waits=[p.at("te")], inc="ve")

        ln_block(hT, ca1_bc, sa_bc, [bc_ready, consts_ready])
        h_done = p.at("ve")
        if debug:
            p.op("sync", lambda e: e.dma_start(out=dbg["dHT"][:, :, :], in_=hT[:, :, :]),
                 waits=[h_done], inc="gdb", dma=True)
        p.op("sync", lambda e: e.dma_start(
            out=Y.rearrange("(c p) d -> p c d", p=128), in_=x_sb[:, :, :]),
            waits=[h_done], inc="gxs", dma=True)
        x_spilled = p.at("gxs")

        def w_prep(dst, WSRC, stage_t, scale_ga, waits0):
            for half in range(2):
                w = list(waits0) + ([p.at("ve")] if half else [])
                p.op("gpsimd", lambda e, WSRC=WSRC, half=half, stage_t=stage_t: e.dma_start(
                    out=stage_t[:, :, :],
                    in_=WSRC[512 * half:512 * half + 512, :].rearrange(
                        "(c p) d -> p c d", p=128)),
                    waits=w, inc="gst", dma=True)
                ld = p.at("gst")
                for c in range(4):
                    ot = 4 * half + c
                    if scale_ga:
                        p.op("vector", lambda e, c=c, ot=ot, stage_t=stage_t: e.tensor_scalar(
                            stage_t[:, c, :], stage_t[:, c, :],
                            mod_pp[:, 16 + ot:17 + ot], None, ALU.mult),
                            waits=[ld, modpp_ready], inc="ve")
                        ld2 = [p.at("ve")]
                    else:
                        ld2 = [ld]
                    for it in range(8):
                        p.op("tensor", lambda e, c=c, it=it, stage_t=stage_t: e.transpose(
                            ps_t[:, 128 * it:128 * it + 128],
                            stage_t[:, c, 128 * it:128 * it + 128], iden_bf[:, :]),
                            waits=ld2 + ([p.at("ve")] if (it == 0 and (c or half)) else []),
                            inc="te" if it == 7 else None)
                    p.op("vector", lambda e, ot=ot, dst=dst: e.tensor_copy(
                        dst[:, :, 128 * ot:128 * ot + 128], ps_t[:, :]),
                        waits=[p.at("te")], inc="ve")

        w_prep(wqT, WQ, stage, False, [x_spilled, consts_ready])
        wq_ready = p.at("ve")
        w_prep(wkT, WK, stage, False, [p.at("ve")])
        wk_ready = p.at("ve")
        w_prep(woT, WO, stage, True, [p.at("ve")])
        wo_ready = p.at("ve")

        evac_marks = [None] * 4

        def proj_out_T(dst_T, wT, bias_pp, waits0):
            g = 0
            for ot in range(8):
                for tc in range(4):
                    bank = g % 4
                    w = list(waits0)
                    if evac_marks[bank] is not None:
                        w.append(evac_marks[bank])
                    for it in range(8):
                        p.op("tensor", lambda e, ot=ot, tc=tc, it=it, bank=bank, wT=wT: e.matmul(
                            ps_a[:, 512 * bank:512 * bank + 512],
                            wT[:, it, 128 * ot:128 * ot + 128],
                            hT[:, it, 512 * tc:512 * tc + 512],
                            start=(it == 0), stop=(it == 7)),
                            waits=w if it == 0 else (), inc="te" if it == 7 else None)
                    mmd = p.at("te")
                    p.op("vector", lambda e, ot=ot, tc=tc, bank=bank, dst_T=dst_T,
                         bias_pp=bias_pp: e.tensor_scalar(
                        dst_T[:, ot, 512 * tc:512 * tc + 512],
                        ps_a[:, 512 * bank:512 * bank + 512],
                        bias_pp[:, ot:ot + 1], None, ALU.add),
                        waits=[mmd], inc="ve")
                    evac_marks[bank] = p.at("ve")
                    g += 1

        proj_out_T(qT, wqT, bq_pp, [wq_ready, h_done, small_ld])
        w_prep(wvT, WV, stage_v, False, [p.at("ve")])
        wv_ready = p.at("ve")
        proj_out_T(kT, wkT, bk_pp, [wk_ready])

        for tt in range(16):
            for oc in range(2):
                bank = (tt * 2 + oc) % 4
                w = [wv_ready, cast_small]
                if evac_marks[bank] is not None:
                    w.append(evac_marks[bank])
                for it in range(8):
                    p.op("tensor", lambda e, tt=tt, oc=oc, it=it, bank=bank: e.matmul(
                        ps_a[:, 512 * bank:512 * bank + 512],
                        hT[:, it, 128 * tt:128 * tt + 128],
                        wvT[:, it, 512 * oc:512 * oc + 512],
                        start=(it == 0), stop=False),
                        waits=w if it == 0 else (), inc=None)
                p.op("tensor", lambda e, oc=oc, bank=bank: e.matmul(
                    ps_a[:, 512 * bank:512 * bank + 512],
                    ones_row_bf[0:1, :], bv_row[0:1, 512 * oc:512 * oc + 512],
                    start=False, stop=True), inc="te")
                mmd = p.at("te")
                p.op("vector", lambda e, tt=tt, oc=oc, bank=bank: e.tensor_copy(
                    v_sb[:, tt, 512 * oc:512 * oc + 512],
                    ps_a[:, 512 * bank:512 * bank + 512]),
                    waits=[mmd], inc="ve")
                evac_marks[bank] = p.at("ve")
        v_done = p.at("ve")
        if debug:
            p.op("sync", lambda e: e.dma_start(out=dbg["dQT"][:, :, :], in_=qT[:, :, :]),
                 waits=[v_done], inc="gdb", dma=True)
            p.op("sync", lambda e: e.dma_start(out=dbg["dKT"][:, :, :], in_=kT[:, :, :]),
                 waits=[v_done], inc="gdb", dma=True)
            p.op("sync", lambda e: e.dma_start(out=dbg["dV"][:, :, :], in_=v_sb[:, :, :]),
                 waits=[v_done], inc="gdb", dma=True)

        gc_used = [None, None]
        mrow_prev = None
        exp_prev = ("se", 0)
        graw_used = [None, None]
        mrow_used = [None, None]
        pass1_ps_free = None
        pvc_free = None
        psb_free = None

        for h in range(NH):
            hp = h % 2
            hb = 64 * (h % 2)
            hc = h // 2
            gw_w = [v_done] if h < 2 else [gc_used[hp]]
            p.op("sync", lambda e, h=h, hp=hp: e.dma_start(
                out=graw[:, hp, :, :],
                in_=GB[h].rearrange("k p f -> p k f")),
                waits=[w for w in gw_w if w], inc="gg0" if hp == 0 else "gg1",
                dma=True)
            gld = p.at("gg0" if hp == 0 else "gg1")
            for k in range(4):
                p.op("gpsimd", lambda e, hp=hp, k=k, h=h: e.tensor_scalar(
                    gcomb[:, hp, k, :], graw[:, hp, k, :], 0.125, nr31[:, h:h + 1],
                    ALU.mult, ALU.add), waits=[gld, nr31_ready], inc="ge")
            p.op("gpsimd", lambda e, hp=hp: e.tensor_tensor(
                gcomb[:, hp, 0, :], gcomb[:, hp, 0, :], mask1[:, :], ALU.add),
                waits=[consts_ready], inc="ge")
            p.op("gpsimd", lambda e, hp=hp: e.tensor_tensor(
                gcomb[:, hp, 2, :], gcomb[:, hp, 2, :], mask1T[:, :], ALU.add),
                inc="ge")
            gc_ready = p.at("ge")
            if debug and h == 0:
                p.op("sync", lambda e: e.dma_start(out=dbg["dGC"][:, :, :, :],
                                                   in_=gcomb[:, :, :, :]),
                     waits=[gc_ready], inc="gdb", dma=True)

            for qt in range(16):
                C = qt // 4 + 1
                w0 = [gc_ready, v_done]
                if pass1_ps_free is not None:
                    w0.append(pass1_ps_free)
                for jc in range(C):
                    W = 512 if jc < C - 1 else 128 * (qt % 4) + 128
                    p.op("tensor", lambda e, qt=qt, jc=jc, W=W, hb=hb, hc=hc: e.matmul(
                        ps_a[:, 512 * jc:512 * jc + W],
                        qT[hb:hb + 64, hc, 128 * qt:128 * qt + 128],
                        kT[hb:hb + 64, hc, 512 * jc:512 * jc + W],
                        start=True, stop=False, skip_group_check=True),
                        waits=w0 if jc == 0 else (), inc=None)
                p.op("tensor", lambda e, qt=qt, hp=hp: e.matmul(
                    ps_a[:, 128 * qt:128 * qt + 128], iden_bf[:, :],
                    gcomb[:, hp, 0, :], start=False, stop=(qt == 0),
                    skip_group_check=True), inc="te" if qt == 0 else None)
                if qt > 0:
                    p.op("tensor", lambda e, qt=qt, hp=hp: e.matmul(
                        ps_a[:, 128 * qt - 128:128 * qt], iden_bf[:, :],
                        gcomb[:, hp, 1, :], start=False, stop=True,
                        skip_group_check=True), inc="te")
                mmd = p.at("te")
                p.op("vector", lambda e, qt=qt: e.reduce_max(
                    m_all[:, qt:qt + 1], ps_a[:, 0:128 * qt + 128], axis=AX.X,
                    negate=True), waits=[mmd], inc="ve")
                pass1_ps_free = p.at("ve")
            p.op("tensor", lambda e: e.transpose(ps_c[0:16, 0:128], m_all[:, :],
                                                 iden_f[:, :]),
                 waits=[pass1_ps_free] + ([pvc_free] if pvc_free else []), inc="te")
            mtw = [p.at("te")]
            if mrow_prev is not None:
                mtw.append(mrow_prev)
            p.op("vector", lambda e: e.tensor_copy(mT16[:, :], ps_c[0:16, 0:128]),
                 waits=mtw, inc="ve")
            mtd = p.at("ve")
            mr_w = [mtd]
            if mrow_used[hp] is not None:
                mr_w.append(mrow_used[hp])
            p.op("sync", lambda e, hp=hp: e.dma_start(out=m_row[0:1, hp, :],
                                                      in_=mT16[:, :]),
                 waits=mr_w, inc="gm0" if hp == 0 else "gm1", dma=True)
            mrow_ready = p.at("gm0" if hp == 0 else "gm1")
            if debug:
                p.op("sync", lambda e, h=h: e.dma_start(
                    out=dbg["dMALL"][:, h, :], in_=m_all[:, :]),
                    waits=[mtd], inc="gdb", dma=True)
            pvc_free = mtd

            pvb = hb
            z_free = pass1_ps_free
            for qc in range(4):
                jtmax = 4 * qc + 3
                npairs = (jtmax + 1) // 2
                exp_done = None
                for pi in range(npairs):
                    jt = 2 * pi
                    sp = pi % 2
                    for dd in range(2):
                        jtc = jt + dd
                        w = [mrow_ready]
                        if psb_free is not None:
                            w.append(psb_free)
                        s = jtc - 4 * qc
                        s2 = jtc + 1 - 4 * qc
                        has_s = 0 <= s <= 3
                        has_s2 = 0 <= s2 <= 3
                        p.op("tensor", lambda e, jtc=jtc, dd=dd, qc=qc,
                             hb=hb, hc=hc: e.matmul(
                            ps_b[:, 512 * dd:512 * dd + 512],
                            kT[hb:hb + 64, hc, 128 * jtc:128 * jtc + 128],
                            qT[hb:hb + 64, hc, 512 * qc:512 * qc + 512],
                            start=True, stop=False, skip_group_check=True),
                            waits=w if dd == 0 else (), inc=None)
                        p.op("tensor", lambda e, dd=dd, qc=qc, hp=hp: e.matmul(
                            ps_b[:, 512 * dd:512 * dd + 512],
                            ones_row_bf[0:1, 0:128],
                            m_row[0:1, hp, 512 * qc:512 * qc + 512],
                            start=False, stop=(not has_s and not has_s2),
                            skip_group_check=True),
                            inc=None if (has_s or has_s2) else "te")
                        if has_s:
                            p.op("tensor", lambda e, dd=dd, s=s, hp=hp: e.matmul(
                                ps_b[:, 512 * dd + 128 * s:512 * dd + 128 * s + 128],
                                iden_bf[:, :], gcomb[:, hp, 2, :],
                                start=False, stop=(not has_s2), skip_group_check=True),
                                inc=None if has_s2 else "te")
                        if has_s2:
                            p.op("tensor", lambda e, dd=dd, s2=s2, hp=hp: e.matmul(
                                ps_b[:, 512 * dd + 128 * s2:512 * dd + 128 * s2 + 128],
                                iden_bf[:, :], gcomb[:, hp, 3, :],
                                start=False, stop=True, skip_group_check=True),
                                inc="te")
                    mmd = p.at("te")
                    p.op("vector", lambda e: e.tensor_copy(
                        pexp_scr[:, :], ps_b[:, :]), waits=[mmd, exp_prev], inc="ve")
                    psb_free = p.at("ve")
                    p.op("scalar", lambda e, sp=sp: e.activation(
                        ptile[:, 2 * sp:2 * sp + 2, :], pexp_scr[:, :], AF.Exp),
                        waits=[psb_free], inc="se")
                    exp_done = p.at("se")
                    exp_prev = exp_done
                    psb_free = exp_done
                    for dd in range(2):
                        jtc = jt + dd
                        off = 128 * max(0, jtc - 4 * qc)
                        w = [exp_done]
                        if jtc == 0 and pvc_free is not None:
                            w.append(pvc_free)
                        p.op("tensor", lambda e, jtc=jtc, dd=dd, sp=sp, off=off,
                             hc=hc: e.matmul(
                            ps_c[:, off:512],
                            v_sb[:, jtc, 128 * hc:128 * hc + 128],
                            ptile[:, 2 * sp + dd, off:512],
                            start=(jtc == 0), stop=False, skip_group_check=True),
                            waits=w, inc=None)
                        zw = []
                        if jtc == 0:
                            zw.append(z_free)
                        p.op("tensor", lambda e, dd=dd, sp=sp, off=off, jtc=jtc: e.matmul(
                            ps_a[0:1, off:512], ones_col_bf[:, :],
                            ptile[:, 2 * sp + dd, off:512],
                            start=(jtc == 0), stop=False, skip_group_check=True),
                            waits=zw, inc="te")
                pv_done = p.at("te")
                rb_w = [pv_done]
                if mrow_used[hp] is not None:
                    pass
                p.op("vector", lambda e, hp=hp: e.reciprocal(
                    r_bf[0:1, hp, 0:512], ps_a[0:1, 0:512]),
                    waits=rb_w, inc="ve")
                rd = p.at("ve")
                z_free = rd
                pass1_ps_free = rd
                p.op("tensor", lambda e, hp=hp: e.matmul(
                    ps_b[:, 0:512], ones_row_bf[0:1, 0:128],
                    r_bf[0:1, hp, 0:512], start=True, stop=True),
                    waits=[rd, psb_free] if psb_free else [rd], inc="te")
                p.op("vector", lambda e, pvb=pvb: e.tensor_copy(
                    rbc_sb[pvb:pvb + 64, :], ps_b[pvb:pvb + 64, 0:512]),
                    waits=[p.at("te")], inc="ve")
                psb_free = p.at("ve")
                p.op("vector", lambda e, pvb=pvb, hc=hc, qc=qc: e.tensor_tensor(
                    oT[pvb:pvb + 64, hc, 512 * qc:512 * qc + 512],
                    ps_c[pvb:pvb + 64, 0:512], rbc_sb[pvb:pvb + 64, :], ALU.mult),
                    inc="ve")
                pvc_free = p.at("ve")
            gc_used[hp] = p.at("te")
            mrow_used[hp] = p.at("te")
            mrow_prev = mrow_ready

        attn_done = p.at("ve")
        if debug:
            p.op("sync", lambda e: e.dma_start(out=dbg["dOT"][:, :, :], in_=oT[:, :, :]),
                 waits=[attn_done], inc="gdb", dma=True)

        p.op("sync", lambda e: e.dma_start(
            out=x_sb[:, :, :], in_=Y.rearrange("(c p) d -> p c d", p=128)),
            waits=[attn_done, x_spilled], inc="gx", dma=True)
        x_back = p.at("gx")
        def bc_build2(dst, src_row, add1, waits0):
            for c in range(2):
                p.op("tensor", lambda e, src_row=src_row, c=c: e.matmul(
                    ps_b[:, 512 * c:512 * c + 512], ones_row_f[:, :],
                    src_row[0:1, 512 * c:512 * c + 512], start=True, stop=True),
                    waits=waits0 if c == 0 else (), inc="te" if c == 1 else None)
            mmd = p.at("te")
            for c in range(2):
                if add1:
                    p.op("vector", lambda e, dst=dst, c=c: e.tensor_scalar(
                        dst[:, 512 * c:512 * c + 512], ps_b[:, 512 * c:512 * c + 512],
                        1.0, None, ALU.add), waits=[mmd], inc="ve")
                else:
                    p.op("vector", lambda e, dst=dst, c=c: e.tensor_copy(
                        dst[:, 512 * c:512 * c + 512], ps_b[:, 512 * c:512 * c + 512]),
                        waits=[mmd], inc="ve")

        bc_build2(cb1_bc, cb_row, True, [attn_done, psb_free])
        bc_build2(sb_bc, sb_row, False, [p.at("ve")])
        cbsb_ready = p.at("ve")

        evac_marks = [None] * 4
        for tt in range(16):
            for oc in range(2):
                bank = (tt * 2 + oc) % 4
                w = [attn_done, wo_ready, bc_ready, x_back]
                if evac_marks[bank] is not None:
                    w.append(evac_marks[bank])
                for it in range(8):
                    p.op("tensor", lambda e, tt=tt, oc=oc, it=it, bank=bank: e.matmul(
                        ps_a[:, 512 * bank:512 * bank + 512],
                        oT[:, it, 128 * tt:128 * tt + 128],
                        woT[:, it, 512 * oc:512 * oc + 512],
                        start=(it == 0), stop=False),
                        waits=w if it == 0 else (), inc=None)
                p.op("tensor", lambda e, oc=oc, bank=bank: e.matmul(
                    ps_a[:, 512 * bank:512 * bank + 512], ones_row_bf[0:1, :],
                    gabo_row[0:1, 512 * oc:512 * oc + 512],
                    start=False, stop=True), inc="te")
                mmd = p.at("te")
                p.op("vector", lambda e, tt=tt, oc=oc, bank=bank: e.tensor_tensor(
                    x_sb[:, tt, 512 * oc:512 * oc + 512],
                    x_sb[:, tt, 512 * oc:512 * oc + 512],
                    ps_a[:, 512 * bank:512 * bank + 512], ALU.add),
                    waits=[mmd], inc="ve")
                evac_marks[bank] = p.at("ve")
        x_attn = p.at("ve")
        p.op("sync", lambda e: e.dma_start(out=b1pp[:, :, :], in_=B1P[:, :, :]),
             waits=[p.at("te")], inc="gsm", dma=True)
        b1_ld = p.at("gsm")
        if debug:
            p.op("sync", lambda e: e.dma_start(out=dbg["dXA"][:, :, :], in_=x_sb[:, :, :]),
                 waits=[x_attn], inc="gdb", dma=True)

        ln_block(h2T, cb1_bc, sb_bc, [x_attn, cbsb_ready])
        h2_done = p.at("ve")
        if debug:
            p.op("sync", lambda e: e.dma_start(out=dbg["dH2T"][:, :, :], in_=h2T[:, :, :]),
                 waits=[h2_done], inc="gdb", dma=True)

        for tt in range(16):
            w = [h2_done, cast_small, pvc_free]
            for dt in range(8):
                p.op("tensor", lambda e, tt=tt, dt=dt: e.matmul(
                    ps_c[:, 0:NE], h2T[:, dt, 128 * tt:128 * tt + 128],
                    wg_sb[:, dt, :], start=(dt == 0), stop=(dt == 7)),
                    waits=w if dt == 0 else (), inc="te" if dt == 7 else None)
            mmd = p.at("te")
            p.op("vector", lambda e: e.reduce_max(g_m1n[:, :], ps_c[:, 0:NE], axis=AX.X,
                                                  negate=True), waits=[mmd], inc="ve")
            p.op("vector", lambda e: e.tensor_scalar(
                g_eq1[:, :], ps_c[:, 0:NE], -1.0, g_m1n[:, 0:1], ALU.mult,
                ALU.is_equal), waits=[p.at("ve")], inc="ve")
            p.op("vector", lambda e: e.tensor_scalar(
                g_msk[:, :], g_eq1[:, :], -1e9, None, ALU.mult), inc="ve")
            p.op("vector", lambda e: e.tensor_tensor(
                g_msk[:, :], g_msk[:, :], ps_c[:, 0:NE], ALU.add), inc="ve")
            gv = p.at("ve")
            p.op("vector", lambda e: e.reduce_max(g_m2n[:, :], g_msk[:, :], axis=AX.X,
                                                  negate=True), waits=[gv], inc="ve")
            pvc_free = p.at("ve")
            p.op("scalar", lambda e: e.activation(g_d[:, :], g_m2n[:, :], AF.Exp,
                                                  bias=g_m1n[:, 0:1], scale=-1.0),
                 waits=[p.at("ve")], inc="se")
            p.op("vector", lambda e: e.tensor_scalar(g_s[:, :], g_d[:, :], 1.0, None,
                                                     ALU.add), waits=[p.at("se")], inc="ve")
            p.op("vector", lambda e: e.reciprocal(g_r[:, :], g_s[:, :]),
                 waits=[p.at("ve")], inc="ve")
            p.op("vector", lambda e: e.tensor_tensor(g_w2[:, :], g_d[:, :], g_r[:, :],
                                                     ALU.mult), waits=[p.at("ve")], inc="ve")
            p.op("vector", lambda e: e.tensor_scalar(
                g_eq2[:, :], g_msk[:, :], -1.0, g_m2n[:, 0:1], ALU.mult, ALU.is_equal),
                inc="ve")
            p.op("vector", lambda e, tt=tt: e.tensor_scalar(
                coef[:, tt, :], g_eq1[:, :], g_r[:, 0:1], None, ALU.mult), inc="ve")
            p.op("vector", lambda e, tt=tt: e.scalar_tensor_tensor(
                coef[:, tt, :], g_eq2[:, :], g_w2[:, 0:1], coef[:, tt, :],
                ALU.mult, ALU.add), inc="ve")
            if debug and tt == 0:
                p.op("sync", lambda e: e.dma_start(out=dbg["dWG"][:, :, :],
                                                   in_=wg_sb[:, :, :]),
                     inc="gdb", dma=True)
                for k, src in enumerate([g_eq1, g_eq2, g_msk]):
                    p.op("vector", lambda e, k=k, src=src: e.tensor_copy(
                        scr[:, 40 * 0 + 8 * k:8 * k + 8], src[:, :]), inc="ve")
                for k, src in enumerate([g_m1n, g_m2n, g_d, g_s, g_r, g_w2]):
                    p.op("vector", lambda e, k=k, src=src: e.tensor_copy(
                        scr[:, 24 + k:25 + k], src[:, :]), inc="ve")
                p.op("sync", lambda e: e.dma_start(out=dbg["dGI"][:, 0:40],
                                                   in_=scr[:, 0:40]),
                     waits=[p.at("ve")], inc="gdb", dma=True)
        coef_done = p.at("ve")
        if debug:
            p.op("sync", lambda e: e.dma_start(out=dbg["dCOEF"][:, :, :],
                                               in_=coef[:, :, :]),
                 waits=[coef_done], inc="gdb", dma=True)

        p.op("sync", lambda e: e.dma_start(
            out=Y.rearrange("(c p) d -> p c d", p=128), in_=x_sb[:, :, :]),
            waits=[x_attn, h2_done], inc="gxs", dma=True)
        x_spill2 = p.at("gxs")

        w1_marks = [None] * 4
        m1_evac = [None] * 4
        m2_marks = [None, None, None]
        ff_marks = [None, None]
        acc_region = {}
        w2_used_te = None
        hs_used_te = None
        b2_used_te = None

        def m2slot(i):
            j = i % 3
            if j == 0:
                return ps_b[:, 0:512]
            if j == 1:
                return ps_b[:, 512:1024]
            return ps_c[:, 0:512]

        for e_i in range(NE):
            b2w = [x_spill2] if e_i == 0 else [b2_used_te]
            p.op("gpsimd", lambda e, e_i=e_i: e.dma_start(
                out=b2row[:, :], in_=B2I[e_i:e_i + 1, :]),
                waits=b2w, inc="gb2", dma=True)
            b2_ld = p.at("gb2")
            w2_chunks = []
            for fb in range(8):
                w2_chunks.append((fb,))
            w2_ld_done = None
            w2q = list(w2_chunks)

            def emit_w2_chunk():
                fb, = w2q.pop(0)
                w = []
                if e_i > 0 and w2_used_te is not None:
                    w.append(w2_used_te)
                if e_i == 0:
                    w.append(coef_done)
                p.op("gpsimd", lambda e, e_i=e_i, fb=fb: e.dma_start(
                    out=w2sb[:, 4 * fb:4 * fb + 4, :],
                    in_=W2I[e_i, 512 * fb:512 * fb + 512, :].rearrange(
                        "(c p) d -> p c d", p=128)),
                    waits=w, inc="gw2", dma=True)

            for tc in range(4):
                for ft in range(32):
                    slot = ft % 4
                    w = []
                    if w1_marks[slot] is not None:
                        w.append(w1_marks[slot])
                    if e_i == 0 and tc == 0 and ft == 0:
                        w.append(x_spill2)
                    w1sem = ["gwa", "gwb", "gwc", "gwd"][slot]
                    p.op("gpsimd", lambda e, e_i=e_i, ft=ft, slot=slot: e.dma_start(
                        out=w1buf[:, slot, :, :],
                        in_=W1TI[e_i, ft].rearrange("dt p f -> p dt f")),
                        waits=w, inc=w1sem, dma=True)
                    w1_ld = p.at(w1sem)
                    if tc == 0 and ft % 4 == 0 and w2q:
                        emit_w2_chunk()
                    bank = ft % 4
                    mw = [w1_ld, h2_done]
                    if m1_evac[bank] is not None:
                        mw.append(m1_evac[bank])
                    if hs_used_te is not None:
                        mw.append(hs_used_te)
                    for dt in range(8):
                        p.op("tensor", lambda e, ft=ft, dt=dt, tc=tc, slot=slot,
                             bank=bank: e.matmul(
                            ps_a[:, 512 * bank:512 * bank + 512],
                            w1buf[:, slot, dt, :],
                            h2T[:, dt, 512 * tc:512 * tc + 512],
                            start=(dt == 0), stop=(dt == 7)),
                            waits=mw if dt == 0 else (), inc="te" if dt == 7 else None)
                    w1_marks[slot] = p.at("te")
                    mmd = p.at("te")
                    p.op("vector", lambda e, ft=ft, bank=bank, e_i=e_i: e.tensor_scalar(
                        hslab[:, ft, :], ps_a[:, 512 * bank:512 * bank + 512],
                        b1pp[:, e_i, ft:ft + 1], zero_col[:, 0:1], ALU.add, ALU.max),
                        waits=[mmd, b1_ld], inc="ve")
                    m1_evac[bank] = p.at("ve")
                m1_done = p.at("ve")
                while tc == 0 and w2q:
                    emit_w2_chunk()
                if tc == 0:
                    w2_ld_done = p.at("gw2")
                for g in range(8):
                    ttl = g // 2
                    dc = g % 2
                    ttg = 4 * tc + ttl
                    si = g % 3
                    w = [m1_done, b2_ld, w2_ld_done, coef_done]
                    if m2_marks[si] is not None:
                        w.append(m2_marks[si])
                    p.op("tensor", lambda e, dc=dc, si=si: e.matmul(
                        m2slot(si), ones_row_bf[0:1, :],
                        b2row[0:1, 512 * dc:512 * dc + 512],
                        start=True, stop=False), waits=w, inc=None)
                    for ft in range(32):
                        p.op("tensor", lambda e, ft=ft, ttl=ttl, dc=dc, si=si: e.matmul(
                            m2slot(si), hslab[:, ft, 128 * ttl:128 * ttl + 128],
                            w2sb[:, ft, 512 * dc:512 * dc + 512],
                            start=False, stop=(ft == 31)),
                            inc="te" if ft == 31 else None)
                    mmd = p.at("te")
                    fw = [mmd]
                    if ff_marks[g % 2] is not None:
                        fw.append(ff_marks[g % 2])
                    p.op("vector", lambda e, g=g, ttg=ttg, e_i=e_i, si=si: e.tensor_scalar(
                        ffslot[:, g % 2, :], m2slot(si), coef[:, ttg, e_i:e_i + 1],
                        None, ALU.mult), waits=fw, inc="ve")
                    m2_marks[si] = p.at("ve")
                    gasem = "ga0" if g % 2 == 0 else "ga1"
                    accw = [m2_marks[si]]
                    if (ttg, dc) in acc_region:
                        accw.append(acc_region[(ttg, dc)])
                    p.op("gpsimd", lambda e, g=g, ttg=ttg, dc=dc, e_i=e_i: e.dma_start(
                        out=ACC[128 * ttg:128 * ttg + 128, 512 * dc:512 * dc + 512],
                        in_=ffslot[:, g % 2, :],
                        **({"accum_op": ALU.add} if e_i > 0 else {})),
                        waits=accw, inc=gasem, dma=True)
                    ff_marks[g % 2] = p.at(gasem)
                    acc_region[(ttg, dc)] = p.at(gasem)
                hs_used_te = p.at("te")
                if debug and e_i == 0 and tc == 0:
                    p.op("sync", lambda e: e.dma_start(out=dbg["dHS"][:, :, :],
                                                       in_=hslab[:, :, :]),
                         waits=[m1_done], inc="gdb", dma=True)
                    p.op("sync", lambda e: e.dma_start(out=dbg["dB1"][:, :, :],
                                                       in_=b1pp[:, :, :]),
                         inc="gdb", dma=True)
            w2_used_te = p.at("te")
            b2_used_te = p.at("te")
        acc_done0 = p.at("ga0")
        acc_done1 = p.at("ga1")

        fcw = [None, None]
        fc_ve = [None, None]
        for tt in range(16):
            sl = tt % 2
            w = [acc_done0, acc_done1, x_spill2]
            if fcw[sl] is not None:
                w.append(fcw[sl])
            if fc_ve[sl] is not None:
                w.append(fc_ve[sl])
            p.op("sync", lambda e, tt=tt, sl=sl: e.dma_start(
                out=fc_x[:, sl, :], in_=Y[128 * tt:128 * tt + 128, :]),
                waits=w, inc="gfl", dma=True)
            p.op("sync", lambda e, tt=tt, sl=sl: e.dma_start(
                out=fc_a[:, sl, :], in_=ACC[128 * tt:128 * tt + 128, :]),
                inc="gfl", dma=True)
            ld = p.at("gfl")
            p.op("vector", lambda e, sl=sl: e.tensor_tensor(
                fc_a[:, sl, :], fc_a[:, sl, :], gb_bc[:, :], ALU.mult),
                waits=[ld, bc_ready], inc="ve")
            p.op("vector", lambda e, sl=sl: e.tensor_tensor(
                fc_o[:, sl, :], fc_a[:, sl, :], fc_x[:, sl, :], ALU.add), inc="ve")
            od = p.at("ve")
            fssem = "gfs0" if sl == 0 else "gfs1"
            p.op("sync", lambda e, tt=tt, sl=sl: e.dma_start(
                out=Y[128 * tt:128 * tt + 128, :], in_=fc_o[:, sl, :]),
                waits=[od], inc=fssem, dma=True)
            fcw[sl] = p.at(fssem)
            fc_ve[sl] = od
        finals = [("gfs0", p.cnt["gfs0"]), ("gfs1", p.cnt["gfs1"])]
        if debug:
            finals.append(("gdb", p.cnt["gdb"]))
        p.op("sync", lambda e: e.nop(), waits=finals, inc=None)

        with nc.allow_low_precision(reason="bf16 operands are deliberate; accumulations stay fp32"):
            p.emit(block, limit=limit)
        build_nc.total_ops = p.order
    return nc


_NC_CACHE = {}


def _get_nc(debug=False):
    if debug not in _NC_CACHE:
        _NC_CACHE[debug] = build_nc(debug)
    return _NC_CACHE[debug]


def make_in_maps(inputs):
    x = np.asarray(inputs["L0_attn_x"])
    wq = np.ascontiguousarray(inputs["L0_attn_Wq"], dtype=np.float32)
    wk = np.ascontiguousarray(inputs["L0_attn_Wk"], dtype=np.float32)
    wv = np.ascontiguousarray(inputs["L0_attn_Wv"], dtype=np.float32)
    wo = np.ascontiguousarray(inputs["L0_attn_Wo"], dtype=np.float32)
    bq = np.asarray(inputs["L0_attn_bq"], dtype=np.float32)
    bk = np.asarray(inputs["L0_attn_bk"], dtype=np.float32)
    bv = np.asarray(inputs["L0_attn_bv"], dtype=np.float32)
    bo = np.asarray(inputs["L0_attn_bo"], dtype=np.float32)
    rel = np.asarray(inputs["L0_attn_rel_emb"], dtype=np.float32)
    wg = np.asarray(inputs["L0_ffn_Wg"], dtype=np.float32)
    w1 = np.asarray(inputs["L0_ffn_W1"], dtype=np.float32)
    b1 = np.asarray(inputs["L0_ffn_b1"], dtype=np.float32)
    w2 = np.asarray(inputs["L0_ffn_W2"], dtype=np.float32)
    b2 = np.asarray(inputs["L0_ffn_b2"], dtype=np.float32)
    c = np.asarray(inputs["L0_c"], dtype=np.float32)
    wmod = np.ascontiguousarray(inputs["L0_wmod"], dtype=np.float32)

    shared = {
        "WQ": wq, "WK": wk, "WV": wv, "WO": wo,
        "BQP": np.ascontiguousarray(bq.reshape(8, 128).T),
        "BKP": np.ascontiguousarray(bk.reshape(8, 128).T),
        "BVROW": np.ascontiguousarray(bv[None, :]),
        "BOROW": np.ascontiguousarray(bo[None, :]),
        "GB": _gather_bias_tiles(rel),
        "REL31": np.ascontiguousarray(rel[31:32, :]),
        "WGP": np.ascontiguousarray(wg.reshape(8, 128, NE).transpose(1, 0, 2)),
        "W1TI": np.ascontiguousarray(
            w1.reshape(NE, 8, 128, 32, 128).transpose(0, 3, 1, 2, 4)),
        "B1P": np.ascontiguousarray(b1.reshape(NE, 32, 128).transpose(2, 0, 1)),
        "W2I": np.ascontiguousarray(w2),
        "B2I": np.ascontiguousarray(b2),
        "WMOD": wmod,
    }
    in_maps = []
    for b in range(8):
        m = dict(shared)
        m["X"] = np.ascontiguousarray(x[b], dtype=np.float32)
        m["CP"] = np.ascontiguousarray(c[b].reshape(8, 128).T)
        in_maps.append(m)
    return in_maps


def kernel(**inputs):
    nc = _get_nc(False)
    in_maps = make_in_maps(inputs)
    res = run_bass_kernel_spmd(nc, in_maps, core_ids=list(range(8)))
    out = np.stack([np.asarray(r["Y"]) for r in res.results], axis=0)
    return out.astype(np.float32)


# version 02
# speedup vs baseline: 1.0259x; 1.0259x over previous
import math
import numpy as np
import ml_dtypes
import concourse.bass as bass
import concourse.mybir as mybir
from concourse.bass_utils import run_bass_kernel_spmd

F32 = mybir.dt.float32
BF16 = mybir.dt.bfloat16
AF = mybir.ActivationFunctionType
ALU = mybir.AluOpType
AX = mybir.AxisListType

T = 2048
D = 1024
NH = 16
FF = 4096
NE = 8
EPS = 1e-5

ENGINES = ("tensor", "vector", "scalar", "gpsimd", "sync")


def _bucket_np(n):
    n = np.asarray(n)
    nf = np.maximum(n, 1).astype(np.float32) / np.float32(16.0)
    val = 16 + (np.log(nf) / np.float32(math.log(8.0)) * np.float32(16.0)).astype(np.int32)
    val = np.minimum(val, 31)
    return np.where(n < 16, n, val).astype(np.int64)


def _gather_bias_tiles(rel_emb):
    a = np.arange(128)
    qp, jj = a[:, None], a[None, :]
    idx = np.stack([
        _bucket_np(np.maximum(qp - jj, 0)),
        _bucket_np(qp - jj + 128),
        _bucket_np(np.maximum(jj - qp, 0)),
        _bucket_np(jj - qp + 128),
    ])
    g = np.asarray(rel_emb)[idx]
    return np.ascontiguousarray(g.transpose(3, 0, 1, 2)).astype(np.float32)


class Plan:
    def __init__(self, nc, sems):
        self.nc = nc
        self.sems = sems
        self.cnt = {k: 0 for k in sems}
        self.ops = {e: [] for e in ENGINES}
        self.order = 0

    def at(self, sem):
        return (sem, self.cnt[sem])

    ESEM = {"vector": "ve", "scalar": "se", "gpsimd": "ge"}

    def op(self, engine, fn, waits=(), inc=None, dma=False):
        waits = list(waits)
        if engine in self.ESEM and not dma:
            es = self.ESEM[engine]
            waits.append((es, self.cnt[es]))
            if inc is None:
                inc = es
        for s, v in waits:
            assert v <= self.cnt[s], f"wait on future {s}:{v} > {self.cnt[s]}"
        incs = None
        if inc is not None:
            amt = 16 if dma else 1
            self.cnt[inc] += amt
            incs = (inc, amt)
        import inspect as _insp
        ln = _insp.currentframe().f_back.f_lineno
        self.meta = getattr(self, "meta", [])
        self.meta.append((self.order, engine, ln))
        self.ops[engine].append((fn, tuple(waits), incs, self.order))
        self.order += 1

    def emit(self, block, limit=None):
        sems, ops = self.sems, self.ops

        def runner(name):
            def _run(eng):
                last = {}
                for fn, waits, incs, order in ops[name]:
                    if limit is not None and order >= limit:
                        continue
                    for s, v in waits:
                        if v > 0 and last.get(s, -1) < v:
                            eng.wait_ge(sems[s], v)
                            last[s] = v
                    instr = fn(eng)
                    if incs is not None:
                        instr.then_inc(sems[incs[0]], incs[1])
            return _run

        block.tensor(runner("tensor"))
        block.vector(runner("vector"))
        block.scalar(runner("scalar"))
        block.gpsimd(runner("gpsimd"))
        block.sync(runner("sync"))


def build_nc(debug=False, limit=None):
    from contextlib import ExitStack
    nc = bass.Bass("TRN2", target_bir_lowering=False, detect_race_conditions=False)

    def din(name, shape, dtype=F32):
        return nc.dram_tensor(name, shape, dtype, kind="ExternalInput").ap()

    X = din("X", [T, D])
    WQ = din("WQ", [D, D]); WK = din("WK", [D, D])
    WV = din("WV", [D, D]); WO = din("WO", [D, D])
    BQP = din("BQP", [128, 8]); BKP = din("BKP", [128, 8])
    BVROW = din("BVROW", [1, D]); BOROW = din("BOROW", [1, D])
    GB = din("GB", [NH, 4, 128, 128])
    REL31 = din("REL31", [1, NH])
    WGP = din("WGP", [128, 8, NE])
    W1TI = din("W1TI", [NE, 32, 8, 128, 128])
    B1P = din("B1P", [128, NE, 32])
    W2I = din("W2I", [NE, FF, D])
    B2I = din("B2I", [NE, D])
    CP = din("CP", [128, 8])
    WMOD = din("WMOD", [D, 6 * D])
    Y = nc.dram_tensor("Y", [T, D], F32, kind="ExternalOutput").ap()
    ACC = nc.dram_tensor("ACC", [T, D], F32, kind="Internal").ap()

    dbg = {}
    if debug:
        for name, shape, dt in [
            ("dMOD", [1, 6 * D], F32), ("dHT", [128, 8, T], BF16),
            ("dQT", [128, 8, T], BF16), ("dKT", [128, 8, T], BF16),
            ("dV", [128, 16, D], BF16), ("dMALL", [128, NH, 16], F32),
            ("dOT", [128, 8, T], BF16), ("dXA", [128, 16, D], F32),
            ("dH2T", [128, 8, T], BF16), ("dCOEF", [128, 16, NE], F32),
            ("dHS", [128, 32, 512], BF16), ("dGC", [128, 2, 4, 128], BF16),
            ("dGI", [128, 40], F32), ("dWG", [128, 8, NE], BF16), ("dB1", [128, NE, 32], F32),
        ]:
            dbg[name] = nc.dram_tensor(name, shape, dt, kind="ExternalOutput").ap()

    with ExitStack() as stack:
        ARENA = 212480
        backing = stack.enter_context(
            nc.sbuf_tensor("arena", [128, ARENA], mybir.dt.uint8))
        base_addr = nc.lookup_mloc(backing).addr

        def place(name, shape, dtype, off):
            per = 1
            for s in shape[1:]:
                per *= s
            per *= mybir.dt.size(dtype)
            assert off + per <= ARENA, f"{name} overflows arena: {off}+{per}"
            h = nc.virtual_sbuf_tensor(name, shape, dtype)
            nc._tensor(h.name, h.shape, h.dtype, type="SB")
            mloc = nc.lookup_mloc(h)
            mloc.allocated = True
            mloc.addr = base_addr + off
            return h

        x_sb = place("x_sb", [128, 16, D], F32, 0)
        stage = place("stage", [128, 4, D], BF16, 0)
        qT = place("qT", [128, 8, T], BF16, 0)
        kT = place("kT", [128, 8, T], BF16, 32768)
        hslab = place("hslab", [128, 32, 512], BF16, 0)
        fc_x = place("fc_x", [128, 2, D], F32, 32768)
        fc_a = place("fc_a", [128, 2, D], F32, 40960)
        fc_o = place("fc_o", [128, 2, D], F32, 49152)
        mod_row = place("mod_row", [1, 6 * D], F32, 65536)
        ca1_bc = place("ca1_bc", [128, D], F32, 90112)
        sa_bc = place("sa_bc", [128, D], F32, 94208)
        v_sb = place("v_sb", [128, 16, D], BF16, 65536)
        stage_v = place("stage_v", [128, 4, D], BF16, 65536)
        h2T = place("h2T", [128, 8, T], BF16, 65536)
        hT = place("hT", [128, 8, T], BF16, 98304)
        oT = place("oT", [128, 8, T], BF16, 98304)
        w2sb = place("w2sb", [128, 32, D], BF16, 98304)
        wqT = place("wqT", [128, 8, D], BF16, 131072)
        wvT = place("wvT", [128, 8, D], BF16, 131072)
        bo_f = place("bo_f", [1, D], F32, 131072)
        wkT = place("wkT", [128, 8, D], BF16, 147456)
        wmod_buf = place("wmod_buf", [128, 2, T], F32, 147456)
        ptile = place("ptile", [128, 4, 512], BF16, 131072)
        graw = place("graw", [128, 2, 4, 128], F32, 135168)
        gcomb = place("gcomb", [128, 2, 4, 128], BF16, 139264)
        m_all = place("m_all", [128, 16], F32, 141312)
        mT16 = place("mT16", [16, 128], BF16, 141440)
        m_row = place("m_row", [1, 2, T], BF16, 141824)
        r_bf = place("r_bf", [128, 2, 512], BF16, 150016)
        rbc_sb = place("rbc_sb", [128, 512], F32, 152064)
        pexp_scr = place("pexp_scr", [128, 1024], F32, 154112)
        cb1_bc = place("cb1_bc", [128, D], F32, 154112)
        sb_bc = place("sb_bc", [128, D], F32, 158208)
        woT = place("woT", [128, 8, D], BF16, 163840)
        w1buf = place("w1buf", [128, 4, 8, 128], BF16, 163840)
        b1pp = place("b1pp", [128, NE, 32], F32, 172032)
        b2row = place("b2row", [1, D], BF16, 173056)
        ffslot = place("ffslot", [128, 2, 512], F32, 175104)
        iden_bf = place("iden_bf", [128, 128], BF16, 180224)
        iden_f = place("iden_f", [128, 128], F32, 180480)
        mask1 = place("mask1", [128, 128], BF16, 180992)
        mask1T = place("mask1T", [128, 128], BF16, 181248)
        ones_row_bf = place("ones_row_bf", [128, 128], BF16, 181504)
        ones_row_f = place("ones_row_f", [1, 128], F32, 181760)
        ones_col_bf = place("ones_col_bf", [128, 1], BF16, 182272)
        iota_s = place("iota_s", [128, 128], F32, 182336)
        scr = place("scr", [128, D], F32, 182848)
        h_tile = place("h_tile", [128, D], BF16, 186944)
        stats6 = place("stats6", [128, 12], F32, 188992)
        mv2 = place("mv2", [128, 2], F32, 189056)
        rstd = place("rstd", [128, 1], F32, 189120)
        rr = place("rr", [128, 1], F32, 189124)
        g_eq1 = place("g_eq1", [128, NE], F32, 189184)
        g_eq2 = place("g_eq2", [128, NE], F32, 189216)
        g_msk = place("g_msk", [128, NE], F32, 189248)
        g_m1n = place("g_m1n", [128, 1], F32, 189280)
        g_m2n = place("g_m2n", [128, 1], F32, 189284)
        g_d = place("g_d", [128, 1], F32, 189288)
        g_s = place("g_s", [128, 1], F32, 189292)
        g_r = place("g_r", [128, 1], F32, 189296)
        g_w2 = place("g_w2", [128, 1], F32, 189300)
        coef = place("coef", [128, 16, NE], F32, 189312)
        gb_bc = place("gb_bc", [128, D], F32, 189824)
        cb_row = place("cb_row", [1, D], F32, 193920)
        sb_row = place("sb_row", [1, D], F32, 198016)
        bv_row = place("bv_row", [1, D], BF16, 202112)
        gabo_row = place("gabo_row", [1, D], BF16, 204160)
        bq_pp = place("bq_pp", [128, 8], F32, 206208)
        bk_pp = place("bk_pp", [128, 8], F32, 206240)
        nr31 = place("nr31", [128, NH], F32, 206272)
        rel31row = place("rel31row", [1, NH], F32, 206336)
        c_col = place("c_col", [128, 8], BF16, 206400)
        c_sil = place("c_sil", [128, 8], F32, 206760)
        c_col32 = place("c_col32", [128, 8], F32, 206792)
        wg_sb = place("wg_sb", [128, 8, NE], BF16, 206432)
        mod_pp = place("mod_pp", [128, 48], F32, 206560)
        zero_col = place("zero_col", [128, 1], F32, 206756)
        eps_col = place("eps_col", [128, 1], F32, 206752)

        ps_a = stack.enter_context(nc.psum_tensor("ps_a", [128, 2048], F32))
        ps_b = stack.enter_context(nc.psum_tensor("ps_b", [128, 1024], F32))
        ps_c = stack.enter_context(nc.psum_tensor("ps_c", [128, 512], F32))
        ps_t = stack.enter_context(nc.psum_tensor("ps_t", [128, 1024], BF16))

        semnames = ["te", "ve", "se", "ge", "gx", "gsm", "gw0", "gw1", "gst",
                    "gg0", "gg1", "gm0", "gm1", "gwa", "gwb", "gwc", "gwd", "gsc",
                    "gw2", "gb2", "ga0", "ga1", "gfl", "gfs0", "gfs1", "gxs",
                    "gdb"]
        sems = {n: stack.enter_context(nc.semaphore("s_" + n))
                for n in semnames}
        block = stack.enter_context(nc.Block())
        p = Plan(nc, sems)

        p.op("sync", lambda e: e.dma_start(
            out=x_sb[:, :, :], in_=X.rearrange("(c p) d -> p c d", p=128)),
            inc="gx", dma=True)
        x_loaded = p.at("gx")
        p.op("sync", lambda e: e.dma_start(out=bq_pp[:, :], in_=BQP[:, :]), inc="gsm", dma=True)
        p.op("sync", lambda e: e.dma_start(out=bk_pp[:, :], in_=BKP[:, :]), inc="gsm", dma=True)
        p.op("sync", lambda e: e.dma_start(out=rel31row[:, :], in_=REL31[:, :]), inc="gsm", dma=True)
        p.op("sync", lambda e: e.dma_start(out=bo_f[:, :], in_=BOROW[:, :]), inc="gsm", dma=True)

        p.op("sync", lambda e: e.dma_start(out=c_col32[:, :], in_=CP[:, :]), inc="gsm", dma=True)
        p.op("gpsimd", lambda e: e.dma_start(out=wg_sb[:, :, :], in_=WGP[:, :, :]), inc="gsc", dma=True)
        p.op("gpsimd", lambda e: e.dma_start(out=bv_row[:, :], in_=BVROW[:, :]), inc="gsc", dma=True)
        small_ld = p.at("gsm")
        cast_small = p.at("gsc")


        p.op("vector", lambda e: e.memset(ones_row_bf[:, :], 1.0), inc="ve")
        p.op("vector", lambda e: e.memset(ones_row_f[:, :], 1.0), inc="ve")
        p.op("vector", lambda e: e.memset(ones_col_bf[:, :], 1.0), inc="ve")
        p.op("vector", lambda e: e.memset(eps_col[:, :], EPS), inc="ve")
        p.op("vector", lambda e: e.memset(zero_col[:, :], 0.0), inc="ve")
        p.op("vector", lambda e: e.memset(ps_a[:, :], 0.0), inc="ve")
        p.op("vector", lambda e: e.memset(ps_b[:, :], 0.0), inc="ve")
        p.op("vector", lambda e: e.memset(ps_c[:, :], 0.0), inc="ve")
        psz = p.at("ve")
        p.op("gpsimd", lambda e: e.iota(iota_s[:, :], [[0, 1], [0, 1], [0, 1], [-1, 128]],
                                        base=0, channel_multiplier=1,
                                        allow_small_or_imprecise_dtypes=True), inc="ge")
        io1 = p.at("ge")
        p.op("vector", lambda e: e.tensor_scalar(iden_bf[:, :], iota_s[:, :], 0.0, None,
                                                 ALU.is_equal), waits=[io1], inc="ve")
        p.op("vector", lambda e: e.tensor_scalar(iden_f[:, :], iota_s[:, :], 0.0, None,
                                                 ALU.is_equal), waits=[io1], inc="ve")
        p.op("vector", lambda e: e.tensor_scalar(mask1[:, :], iota_s[:, :], 0.0, -30000.0,
                                                 ALU.is_lt, ALU.mult), waits=[io1], inc="ve")
        p.op("vector", lambda e: e.tensor_scalar(mask1T[:, :], iota_s[:, :], 0.0, -30000.0,
                                                 ALU.is_gt, ALU.mult), waits=[io1], inc="ve")
        consts_ready = p.at("ve")

        p.op("scalar", lambda e: e.activation(c_sil[:, :], c_col32[:, :], AF.Silu),
             waits=[small_ld, cast_small], inc="se")
        csil_ready = p.at("se")

        for ntb in range(3):
            for kt in range(8):
                slot = kt % 2
                w_wait = [p.at("te")] if (ntb or kt >= 2) else []
                p.op("sync", lambda e, ntb=ntb, kt=kt, slot=slot: e.dma_start(
                    out=wmod_buf[:, slot, :],
                    in_=WMOD[128 * kt:128 * kt + 128, 2048 * ntb:2048 * ntb + 2048]),
                    waits=w_wait, inc="gw0" if slot == 0 else "gw1", dma=True)
                ld = p.at("gw0" if slot == 0 else "gw1")
                for nt in range(4):
                    p.op("tensor", lambda e, kt=kt, slot=slot, nt=nt: e.matmul(
                        ps_a[0:1, 512 * nt:512 * nt + 512],
                        c_sil[:, kt:kt + 1], wmod_buf[:, slot, 512 * nt:512 * nt + 512],
                        start=(kt == 0), stop=(kt == 7)),
                        waits=[ld, csil_ready, psz] + ([p.at("ve")] if (ntb and kt == 0 and nt == 0) else []),
                        inc="te" if nt == 3 else None)
            mm_done = p.at("te")
            p.op("vector", lambda e, ntb=ntb: e.tensor_copy(
                mod_row[0:1, 2048 * ntb:2048 * ntb + 2048], ps_a[0:1, 0:2048]),
                waits=[mm_done], inc="ve")
        mod_raw = p.at("ve")

        for off in (2 * D, 5 * D):
            p.op("vector", lambda e, off=off: e.tensor_scalar(
                scr[0:1, 0:D], mod_row[0:1, off:off + D], 0.0, 2.0, ALU.is_ge, ALU.mult),
                waits=[mod_raw], inc="ve")
            p.op("vector", lambda e: e.tensor_scalar(
                scr[0:1, 0:D], scr[0:1, 0:D], -1.0, None, ALU.add), inc="ve")
            p.op("vector", lambda e, off=off: e.tensor_scalar(
                ca1_bc[0:1, 0:D], mod_row[0:1, off:off + D], -1.0, None, ALU.mult),
                inc="ve")
            p.op("vector", lambda e, off=off: e.tensor_tensor(
                mod_row[0:1, off:off + D], mod_row[0:1, off:off + D], ca1_bc[0:1, 0:D],
                ALU.max), inc="ve")
            p.op("vector", lambda e, off=off: e.tensor_scalar(
                mod_row[0:1, off:off + D], mod_row[0:1, off:off + D], 0.25, None,
                ALU.max), inc="ve")
            p.op("vector", lambda e, off=off: e.tensor_tensor(
                mod_row[0:1, off:off + D], mod_row[0:1, off:off + D], scr[0:1, 0:D],
                ALU.mult), inc="ve")
        mod_ready = p.at("ve")
        if debug:
            p.op("sync", lambda e: e.dma_start(out=dbg["dMOD"][:, :], in_=mod_row[:, :]),
                 waits=[mod_ready], inc="gdb", dma=True)

        for c6 in range(48):
            p.op("tensor", lambda e, c6=c6: e.transpose(
                ps_c[:, c6:c6 + 1], mod_row[0:1, 128 * c6:128 * c6 + 128],
                iden_f[0:1, 0:1]),
                waits=[mod_ready, consts_ready], inc="te" if c6 == 47 else None)
        p.op("vector", lambda e: e.tensor_copy(mod_pp[:, :], ps_c[:, 0:48]),
             waits=[p.at("te")], inc="ve")
        modpp_ready = p.at("ve")

        p.op("tensor", lambda e: e.matmul(ps_b[:, 0:NH], ones_row_f[:, :], rel31row[:, :],
                                          start=True, stop=True),
             waits=[small_ld, consts_ready, modpp_ready], inc="te")
        p.op("vector", lambda e: e.tensor_scalar(nr31[:, :], ps_b[:, 0:NH], -0.125, None,
                                                 ALU.mult), waits=[p.at("te")], inc="ve")
        nr31_ready = p.at("ve")

        def bc_build(dst, src_off, add1, wait_extra):
            for c in range(2):
                p.op("tensor", lambda e, src_off=src_off, c=c: e.matmul(
                    ps_b[:, 512 * c:512 * c + 512], ones_row_f[:, :],
                    mod_row[0:1, src_off + 512 * c:src_off + 512 * c + 512],
                    start=True, stop=True),
                    waits=[mod_ready] + wait_extra, inc="te" if c == 1 else None)
            mmd = p.at("te")
            for c in range(2):
                if add1:
                    p.op("vector", lambda e, dst=dst, c=c: e.tensor_scalar(
                        dst[:, 512 * c:512 * c + 512], ps_b[:, 512 * c:512 * c + 512],
                        1.0, None, ALU.add), waits=[mmd], inc="ve")
                else:
                    p.op("vector", lambda e, dst=dst, c=c: e.tensor_copy(
                        dst[:, 512 * c:512 * c + 512], ps_b[:, 512 * c:512 * c + 512]),
                        waits=[mmd], inc="ve")

        bc_build(ca1_bc, 1 * D, True, [nr31_ready])
        bc_build(sa_bc, 0 * D, False, [p.at("ve")])
        bc_build(gb_bc, 5 * D, False, [p.at("ve")])
        p.op("vector", lambda e: e.tensor_copy(cb_row[:, :], mod_row[0:1, 4 * D:5 * D]),
             waits=[mod_ready], inc="ve")
        p.op("vector", lambda e: e.tensor_copy(sb_row[:, :], mod_row[0:1, 3 * D:4 * D]),
             waits=[mod_ready], inc="ve")
        p.op("vector", lambda e: e.tensor_tensor(
            gabo_row[:, :], bo_f[0:1, :], mod_row[0:1, 2 * D:3 * D], ALU.mult),
            waits=[mod_ready, small_ld], inc="ve")
        bc_ready = p.at("ve")

        def ln_block(dst_T, c1bc, sbc, waits0):
            for tt in range(16):
                w = list(waits0) if tt == 0 else [p.at("te")]
                p.op("vector", lambda e, tt=tt: e.bn_stats(
                    stats6[:, 0:6], x_sb[:, tt, 0:512]), waits=w + [x_loaded], inc="ve")
                p.op("vector", lambda e, tt=tt: e.bn_stats(
                    stats6[:, 6:12], x_sb[:, tt, 512:1024]), inc="ve")
                p.op("vector", lambda e: e.bn_aggr(mv2[:, :], stats6[:, :]),
                     waits=[p.at("ve")], inc="ve")
                sd = p.at("ve")
                p.op("scalar", lambda e: e.activation(rstd[:, :], mv2[:, 1:2], AF.Sqrt,
                                                      bias=eps_col[:, 0:1]), waits=[sd], inc="se")
                p.op("vector", lambda e: e.reciprocal(rr[:, :], rstd[:, :]),
                     waits=[p.at("se")], inc="ve")
                p.op("vector", lambda e, tt=tt: e.tensor_scalar(
                    scr[:, :], x_sb[:, tt, :], mv2[:, 0:1], rr[:, 0:1],
                    ALU.subtract, ALU.mult), waits=[p.at("ve")], inc="ve")
                p.op("vector", lambda e, c1bc=c1bc: e.tensor_tensor(
                    scr[:, :], scr[:, :], c1bc[:, :], ALU.mult), inc="ve")
                p.op("vector", lambda e, sbc=sbc: e.tensor_tensor(
                    h_tile[:, :], scr[:, :], sbc[:, :], ALU.add), inc="ve")
                hready = p.at("ve")
                for dt in range(8):
                    p.op("tensor", lambda e, dt=dt: e.transpose(
                        ps_t[:, 128 * dt:128 * dt + 128],
                        h_tile[:, 128 * dt:128 * dt + 128], iden_bf[:, :]),
                        waits=[hready], inc="te" if dt == 7 else None)
                p.op("vector", lambda e, tt=tt, dst_T=dst_T: e.tensor_copy(
                    dst_T[:, :, 128 * tt:128 * tt + 128], ps_t[:, :]),
                    waits=[p.at("te")], inc="ve")

        ln_block(hT, ca1_bc, sa_bc, [bc_ready, consts_ready])
        h_done = p.at("ve")
        if debug:
            p.op("sync", lambda e: e.dma_start(out=dbg["dHT"][:, :, :], in_=hT[:, :, :]),
                 waits=[h_done], inc="gdb", dma=True)
        p.op("sync", lambda e: e.dma_start(
            out=Y.rearrange("(c p) d -> p c d", p=128), in_=x_sb[:, :, :]),
            waits=[h_done], inc="gxs", dma=True)
        x_spilled = p.at("gxs")

        def w_prep(dst, WSRC, stage_t, scale_ga, waits0):
            for half in range(2):
                w = list(waits0) + ([p.at("ve")] if half else [])
                p.op("gpsimd", lambda e, WSRC=WSRC, half=half, stage_t=stage_t: e.dma_start(
                    out=stage_t[:, :, :],
                    in_=WSRC[512 * half:512 * half + 512, :].rearrange(
                        "(c p) d -> p c d", p=128)),
                    waits=w, inc="gst", dma=True)
                ld = p.at("gst")
                for c in range(4):
                    ot = 4 * half + c
                    if scale_ga:
                        p.op("vector", lambda e, c=c, ot=ot, stage_t=stage_t: e.tensor_scalar(
                            stage_t[:, c, :], stage_t[:, c, :],
                            mod_pp[:, 16 + ot:17 + ot], None, ALU.mult),
                            waits=[ld, modpp_ready], inc="ve")
                        ld2 = [p.at("ve")]
                    else:
                        ld2 = [ld]
                    for it in range(8):
                        p.op("tensor", lambda e, c=c, it=it, stage_t=stage_t: e.transpose(
                            ps_t[:, 128 * it:128 * it + 128],
                            stage_t[:, c, 128 * it:128 * it + 128], iden_bf[:, :]),
                            waits=ld2 + ([p.at("ve")] if (it == 0 and (c or half)) else []),
                            inc="te" if it == 7 else None)
                    p.op("vector", lambda e, ot=ot, dst=dst: e.tensor_copy(
                        dst[:, :, 128 * ot:128 * ot + 128], ps_t[:, :]),
                        waits=[p.at("te")], inc="ve")

        w_prep(wqT, WQ, stage, False, [x_spilled, consts_ready])
        wq_ready = p.at("ve")
        w_prep(wkT, WK, stage, False, [p.at("ve")])
        wk_ready = p.at("ve")
        w_prep(woT, WO, stage, True, [p.at("ve")])
        wo_ready = p.at("ve")

        evac_marks = [None] * 4

        def proj_out_T(dst_T, wT, bias_pp, waits0):
            g = 0
            for ot in range(8):
                for tc in range(4):
                    bank = g % 4
                    w = list(waits0)
                    if evac_marks[bank] is not None:
                        w.append(evac_marks[bank])
                    for it in range(8):
                        p.op("tensor", lambda e, ot=ot, tc=tc, it=it, bank=bank, wT=wT: e.matmul(
                            ps_a[:, 512 * bank:512 * bank + 512],
                            wT[:, it, 128 * ot:128 * ot + 128],
                            hT[:, it, 512 * tc:512 * tc + 512],
                            start=(it == 0), stop=(it == 7)),
                            waits=w if it == 0 else (), inc="te" if it == 7 else None)
                    mmd = p.at("te")
                    p.op("vector", lambda e, ot=ot, tc=tc, bank=bank, dst_T=dst_T,
                         bias_pp=bias_pp: e.tensor_scalar(
                        dst_T[:, ot, 512 * tc:512 * tc + 512],
                        ps_a[:, 512 * bank:512 * bank + 512],
                        bias_pp[:, ot:ot + 1], None, ALU.add),
                        waits=[mmd], inc="ve")
                    evac_marks[bank] = p.at("ve")
                    g += 1

        proj_out_T(qT, wqT, bq_pp, [wq_ready, h_done, small_ld])
        w_prep(wvT, WV, stage_v, False, [p.at("ve")])
        wv_ready = p.at("ve")
        proj_out_T(kT, wkT, bk_pp, [wk_ready])

        for tt in range(16):
            for oc in range(2):
                bank = (tt * 2 + oc) % 4
                w = [wv_ready, cast_small]
                if evac_marks[bank] is not None:
                    w.append(evac_marks[bank])
                for it in range(8):
                    p.op("tensor", lambda e, tt=tt, oc=oc, it=it, bank=bank: e.matmul(
                        ps_a[:, 512 * bank:512 * bank + 512],
                        hT[:, it, 128 * tt:128 * tt + 128],
                        wvT[:, it, 512 * oc:512 * oc + 512],
                        start=(it == 0), stop=False),
                        waits=w if it == 0 else (), inc=None)
                p.op("tensor", lambda e, oc=oc, bank=bank: e.matmul(
                    ps_a[:, 512 * bank:512 * bank + 512],
                    ones_row_bf[0:1, :], bv_row[0:1, 512 * oc:512 * oc + 512],
                    start=False, stop=True), inc="te")
                mmd = p.at("te")
                p.op("vector", lambda e, tt=tt, oc=oc, bank=bank: e.tensor_copy(
                    v_sb[:, tt, 512 * oc:512 * oc + 512],
                    ps_a[:, 512 * bank:512 * bank + 512]),
                    waits=[mmd], inc="ve")
                evac_marks[bank] = p.at("ve")
        v_done = p.at("ve")
        if debug:
            p.op("sync", lambda e: e.dma_start(out=dbg["dQT"][:, :, :], in_=qT[:, :, :]),
                 waits=[v_done], inc="gdb", dma=True)
            p.op("sync", lambda e: e.dma_start(out=dbg["dKT"][:, :, :], in_=kT[:, :, :]),
                 waits=[v_done], inc="gdb", dma=True)
            p.op("sync", lambda e: e.dma_start(out=dbg["dV"][:, :, :], in_=v_sb[:, :, :]),
                 waits=[v_done], inc="gdb", dma=True)

        gc_used = [None, None]
        mrow_prev = None
        exp_prev = ("se", 0)
        graw_used = [None, None]
        mrow_used = [None, None]
        pass1_ps_free = None
        pvc_free = None
        psb_free = None

        for h in range(NH):
            hp = h % 2
            hb = 64 * (h % 2)
            hc = h // 2
            gw_w = [v_done] if h < 2 else [gc_used[hp]]
            p.op("sync", lambda e, h=h, hp=hp: e.dma_start(
                out=graw[:, hp, :, :],
                in_=GB[h].rearrange("k p f -> p k f")),
                waits=[w for w in gw_w if w], inc="gg0" if hp == 0 else "gg1",
                dma=True)
            gld = p.at("gg0" if hp == 0 else "gg1")
            for k in range(4):
                p.op("gpsimd", lambda e, hp=hp, k=k, h=h: e.tensor_scalar(
                    gcomb[:, hp, k, :], graw[:, hp, k, :], 0.125, nr31[:, h:h + 1],
                    ALU.mult, ALU.add), waits=[gld, nr31_ready], inc="ge")
            p.op("gpsimd", lambda e, hp=hp: e.tensor_tensor(
                gcomb[:, hp, 0, :], gcomb[:, hp, 0, :], mask1[:, :], ALU.add),
                waits=[consts_ready], inc="ge")
            p.op("gpsimd", lambda e, hp=hp: e.tensor_tensor(
                gcomb[:, hp, 2, :], gcomb[:, hp, 2, :], mask1T[:, :], ALU.add),
                inc="ge")
            gc_ready = p.at("ge")
            if debug and h == 0:
                p.op("sync", lambda e: e.dma_start(out=dbg["dGC"][:, :, :, :],
                                                   in_=gcomb[:, :, :, :]),
                     waits=[gc_ready], inc="gdb", dma=True)

            for qt in range(16):
                C = qt // 4 + 1
                w0 = [gc_ready, v_done]
                if pass1_ps_free is not None:
                    w0.append(pass1_ps_free)
                for jc in range(C):
                    W = 512 if jc < C - 1 else 128 * (qt % 4) + 128
                    p.op("tensor", lambda e, qt=qt, jc=jc, W=W, hb=hb, hc=hc: e.matmul(
                        ps_a[:, 512 * jc:512 * jc + W],
                        qT[hb:hb + 64, hc, 128 * qt:128 * qt + 128],
                        kT[hb:hb + 64, hc, 512 * jc:512 * jc + W],
                        start=True, stop=False, skip_group_check=True),
                        waits=w0 if jc == 0 else (), inc=None)
                p.op("tensor", lambda e, qt=qt, hp=hp: e.matmul(
                    ps_a[:, 128 * qt:128 * qt + 128], iden_bf[:, :],
                    gcomb[:, hp, 0, :], start=False, stop=(qt == 0),
                    skip_group_check=True), inc="te" if qt == 0 else None)
                if qt > 0:
                    p.op("tensor", lambda e, qt=qt, hp=hp: e.matmul(
                        ps_a[:, 128 * qt - 128:128 * qt], iden_bf[:, :],
                        gcomb[:, hp, 1, :], start=False, stop=True,
                        skip_group_check=True), inc="te")
                mmd = p.at("te")
                p.op("vector", lambda e, qt=qt: e.reduce_max(
                    m_all[:, qt:qt + 1], ps_a[:, 0:128 * qt + 128], axis=AX.X,
                    negate=True), waits=[mmd], inc="ve")
                pass1_ps_free = p.at("ve")
            p.op("tensor", lambda e: e.transpose(ps_c[0:16, 0:128], m_all[:, :],
                                                 iden_f[:, :]),
                 waits=[pass1_ps_free] + ([pvc_free] if pvc_free else []), inc="te")
            mtw = [p.at("te")]
            if mrow_prev is not None:
                mtw.append(mrow_prev)
            p.op("vector", lambda e: e.tensor_copy(mT16[:, :], ps_c[0:16, 0:128]),
                 waits=mtw, inc="ve")
            mtd = p.at("ve")
            mr_w = [mtd]
            if mrow_used[hp] is not None:
                mr_w.append(mrow_used[hp])
            p.op("sync", lambda e, hp=hp: e.dma_start(out=m_row[0:1, hp, :],
                                                      in_=mT16[:, :]),
                 waits=mr_w, inc="gm0" if hp == 0 else "gm1", dma=True)
            mrow_ready = p.at("gm0" if hp == 0 else "gm1")
            if debug:
                p.op("sync", lambda e, h=h: e.dma_start(
                    out=dbg["dMALL"][:, h, :], in_=m_all[:, :]),
                    waits=[mtd], inc="gdb", dma=True)
            pvc_free = mtd

            pvb = hb
            z_free = pass1_ps_free
            for qc in range(4):
                jtmax = 4 * qc + 3
                npairs = (jtmax + 1) // 2
                exp_done = None
                for pi in range(npairs):
                    jt = 2 * pi
                    sp = pi % 2
                    for dd in range(2):
                        jtc = jt + dd
                        w = [mrow_ready]
                        if psb_free is not None:
                            w.append(psb_free)
                        s = jtc - 4 * qc
                        s2 = jtc + 1 - 4 * qc
                        has_s = 0 <= s <= 3
                        has_s2 = 0 <= s2 <= 3
                        p.op("tensor", lambda e, jtc=jtc, dd=dd, qc=qc,
                             hb=hb, hc=hc: e.matmul(
                            ps_b[:, 512 * dd:512 * dd + 512],
                            kT[hb:hb + 64, hc, 128 * jtc:128 * jtc + 128],
                            qT[hb:hb + 64, hc, 512 * qc:512 * qc + 512],
                            start=True, stop=False, skip_group_check=True),
                            waits=w if dd == 0 else (), inc=None)
                        p.op("tensor", lambda e, dd=dd, qc=qc, hp=hp: e.matmul(
                            ps_b[:, 512 * dd:512 * dd + 512],
                            ones_row_bf[0:1, 0:128],
                            m_row[0:1, hp, 512 * qc:512 * qc + 512],
                            start=False, stop=(not has_s and not has_s2),
                            skip_group_check=True),
                            inc=None if (has_s or has_s2) else "te")
                        if has_s:
                            p.op("tensor", lambda e, dd=dd, s=s, hp=hp: e.matmul(
                                ps_b[:, 512 * dd + 128 * s:512 * dd + 128 * s + 128],
                                iden_bf[:, :], gcomb[:, hp, 2, :],
                                start=False, stop=(not has_s2), skip_group_check=True),
                                inc=None if has_s2 else "te")
                        if has_s2:
                            p.op("tensor", lambda e, dd=dd, s2=s2, hp=hp: e.matmul(
                                ps_b[:, 512 * dd + 128 * s2:512 * dd + 128 * s2 + 128],
                                iden_bf[:, :], gcomb[:, hp, 3, :],
                                start=False, stop=True, skip_group_check=True),
                                inc="te")
                    mmd = p.at("te")
                    p.op("vector", lambda e: e.tensor_copy(
                        pexp_scr[:, :], ps_b[:, :]), waits=[mmd, exp_prev], inc="ve")
                    psb_free = p.at("ve")
                    p.op("scalar", lambda e, sp=sp: e.activation(
                        ptile[:, 2 * sp:2 * sp + 2, :], pexp_scr[:, :], AF.Exp),
                        waits=[psb_free], inc="se")
                    exp_done = p.at("se")
                    exp_prev = exp_done
                    psb_free = exp_done
                    for dd in range(2):
                        jtc = jt + dd
                        off = 128 * max(0, jtc - 4 * qc)
                        w = [exp_done]
                        if jtc == 0 and pvc_free is not None:
                            w.append(pvc_free)
                        p.op("tensor", lambda e, jtc=jtc, dd=dd, sp=sp, off=off,
                             hc=hc: e.matmul(
                            ps_c[:, off:512],
                            v_sb[:, jtc, 128 * hc:128 * hc + 128],
                            ptile[:, 2 * sp + dd, off:512],
                            start=(jtc == 0), stop=False, skip_group_check=True),
                            waits=w, inc=None)
                        zw = []
                        if jtc == 0:
                            zw.append(z_free)
                        p.op("tensor", lambda e, dd=dd, sp=sp, off=off, jtc=jtc: e.matmul(
                            ps_a[0:1, off:512], ones_col_bf[:, :],
                            ptile[:, 2 * sp + dd, off:512],
                            start=(jtc == 0), stop=False, skip_group_check=True),
                            waits=zw, inc="te")
                pv_done = p.at("te")
                rb_w = [pv_done]
                if mrow_used[hp] is not None:
                    pass
                p.op("vector", lambda e, hp=hp: e.reciprocal(
                    r_bf[0:1, hp, 0:512], ps_a[0:1, 0:512]),
                    waits=rb_w, inc="ve")
                rd = p.at("ve")
                z_free = rd
                pass1_ps_free = rd
                p.op("tensor", lambda e, hp=hp: e.matmul(
                    ps_b[:, 0:512], ones_row_bf[0:1, 0:128],
                    r_bf[0:1, hp, 0:512], start=True, stop=True),
                    waits=[rd, psb_free] if psb_free else [rd], inc="te")
                p.op("vector", lambda e, pvb=pvb: e.tensor_copy(
                    rbc_sb[pvb:pvb + 64, :], ps_b[pvb:pvb + 64, 0:512]),
                    waits=[p.at("te")], inc="ve")
                psb_free = p.at("ve")
                p.op("vector", lambda e, pvb=pvb, hc=hc, qc=qc: e.tensor_tensor(
                    oT[pvb:pvb + 64, hc, 512 * qc:512 * qc + 512],
                    ps_c[pvb:pvb + 64, 0:512], rbc_sb[pvb:pvb + 64, :], ALU.mult),
                    inc="ve")
                pvc_free = p.at("ve")
            gc_used[hp] = p.at("te")
            mrow_used[hp] = p.at("te")
            mrow_prev = mrow_ready

        attn_done = p.at("ve")
        if debug:
            p.op("sync", lambda e: e.dma_start(out=dbg["dOT"][:, :, :], in_=oT[:, :, :]),
                 waits=[attn_done], inc="gdb", dma=True)

        p.op("sync", lambda e: e.dma_start(
            out=x_sb[:, :, :], in_=Y.rearrange("(c p) d -> p c d", p=128)),
            waits=[attn_done, x_spilled], inc="gx", dma=True)
        x_back = p.at("gx")
        def bc_build2(dst, src_row, add1, waits0):
            for c in range(2):
                p.op("tensor", lambda e, src_row=src_row, c=c: e.matmul(
                    ps_b[:, 512 * c:512 * c + 512], ones_row_f[:, :],
                    src_row[0:1, 512 * c:512 * c + 512], start=True, stop=True),
                    waits=waits0 if c == 0 else (), inc="te" if c == 1 else None)
            mmd = p.at("te")
            for c in range(2):
                if add1:
                    p.op("vector", lambda e, dst=dst, c=c: e.tensor_scalar(
                        dst[:, 512 * c:512 * c + 512], ps_b[:, 512 * c:512 * c + 512],
                        1.0, None, ALU.add), waits=[mmd], inc="ve")
                else:
                    p.op("vector", lambda e, dst=dst, c=c: e.tensor_copy(
                        dst[:, 512 * c:512 * c + 512], ps_b[:, 512 * c:512 * c + 512]),
                        waits=[mmd], inc="ve")

        bc_build2(cb1_bc, cb_row, True, [attn_done, psb_free])
        bc_build2(sb_bc, sb_row, False, [p.at("ve")])
        cbsb_ready = p.at("ve")

        evac_marks = [None] * 4
        for tt in range(16):
            for oc in range(2):
                bank = (tt * 2 + oc) % 4
                w = [attn_done, wo_ready, bc_ready, x_back]
                if evac_marks[bank] is not None:
                    w.append(evac_marks[bank])
                for it in range(8):
                    p.op("tensor", lambda e, tt=tt, oc=oc, it=it, bank=bank: e.matmul(
                        ps_a[:, 512 * bank:512 * bank + 512],
                        oT[:, it, 128 * tt:128 * tt + 128],
                        woT[:, it, 512 * oc:512 * oc + 512],
                        start=(it == 0), stop=False),
                        waits=w if it == 0 else (), inc=None)
                p.op("tensor", lambda e, oc=oc, bank=bank: e.matmul(
                    ps_a[:, 512 * bank:512 * bank + 512], ones_row_bf[0:1, :],
                    gabo_row[0:1, 512 * oc:512 * oc + 512],
                    start=False, stop=True), inc="te")
                mmd = p.at("te")
                p.op("vector", lambda e, tt=tt, oc=oc, bank=bank: e.tensor_tensor(
                    x_sb[:, tt, 512 * oc:512 * oc + 512],
                    x_sb[:, tt, 512 * oc:512 * oc + 512],
                    ps_a[:, 512 * bank:512 * bank + 512], ALU.add),
                    waits=[mmd], inc="ve")
                evac_marks[bank] = p.at("ve")
        x_attn = p.at("ve")
        p.op("sync", lambda e: e.dma_start(out=b1pp[:, :, :], in_=B1P[:, :, :]),
             waits=[p.at("te")], inc="gsm", dma=True)
        b1_ld = p.at("gsm")
        if debug:
            p.op("sync", lambda e: e.dma_start(out=dbg["dXA"][:, :, :], in_=x_sb[:, :, :]),
                 waits=[x_attn], inc="gdb", dma=True)

        ln_block(h2T, cb1_bc, sb_bc, [x_attn, cbsb_ready])
        h2_done = p.at("ve")
        if debug:
            p.op("sync", lambda e: e.dma_start(out=dbg["dH2T"][:, :, :], in_=h2T[:, :, :]),
                 waits=[h2_done], inc="gdb", dma=True)

        for tt in range(16):
            w = [h2_done, cast_small, pvc_free]
            for dt in range(8):
                p.op("tensor", lambda e, tt=tt, dt=dt: e.matmul(
                    ps_c[:, 0:NE], h2T[:, dt, 128 * tt:128 * tt + 128],
                    wg_sb[:, dt, :], start=(dt == 0), stop=(dt == 7)),
                    waits=w if dt == 0 else (), inc="te" if dt == 7 else None)
            mmd = p.at("te")
            p.op("vector", lambda e: e.reduce_max(g_m1n[:, :], ps_c[:, 0:NE], axis=AX.X,
                                                  negate=True), waits=[mmd], inc="ve")
            p.op("vector", lambda e: e.tensor_scalar(
                g_eq1[:, :], ps_c[:, 0:NE], -1.0, g_m1n[:, 0:1], ALU.mult,
                ALU.is_equal), waits=[p.at("ve")], inc="ve")
            p.op("vector", lambda e: e.tensor_scalar(
                g_msk[:, :], g_eq1[:, :], -1e9, None, ALU.mult), inc="ve")
            p.op("vector", lambda e: e.tensor_tensor(
                g_msk[:, :], g_msk[:, :], ps_c[:, 0:NE], ALU.add), inc="ve")
            gv = p.at("ve")
            p.op("vector", lambda e: e.reduce_max(g_m2n[:, :], g_msk[:, :], axis=AX.X,
                                                  negate=True), waits=[gv], inc="ve")
            pvc_free = p.at("ve")
            p.op("scalar", lambda e: e.activation(g_d[:, :], g_m2n[:, :], AF.Exp,
                                                  bias=g_m1n[:, 0:1], scale=-1.0),
                 waits=[p.at("ve")], inc="se")
            p.op("vector", lambda e: e.tensor_scalar(g_s[:, :], g_d[:, :], 1.0, None,
                                                     ALU.add), waits=[p.at("se")], inc="ve")
            p.op("vector", lambda e: e.reciprocal(g_r[:, :], g_s[:, :]),
                 waits=[p.at("ve")], inc="ve")
            p.op("vector", lambda e: e.tensor_tensor(g_w2[:, :], g_d[:, :], g_r[:, :],
                                                     ALU.mult), waits=[p.at("ve")], inc="ve")
            p.op("vector", lambda e: e.tensor_scalar(
                g_eq2[:, :], g_msk[:, :], -1.0, g_m2n[:, 0:1], ALU.mult, ALU.is_equal),
                inc="ve")
            p.op("vector", lambda e, tt=tt: e.tensor_scalar(
                coef[:, tt, :], g_eq1[:, :], g_r[:, 0:1], None, ALU.mult), inc="ve")
            p.op("vector", lambda e, tt=tt: e.scalar_tensor_tensor(
                coef[:, tt, :], g_eq2[:, :], g_w2[:, 0:1], coef[:, tt, :],
                ALU.mult, ALU.add), inc="ve")
            if debug and tt == 0:
                p.op("sync", lambda e: e.dma_start(out=dbg["dWG"][:, :, :],
                                                   in_=wg_sb[:, :, :]),
                     inc="gdb", dma=True)
                for k, src in enumerate([g_eq1, g_eq2, g_msk]):
                    p.op("vector", lambda e, k=k, src=src: e.tensor_copy(
                        scr[:, 40 * 0 + 8 * k:8 * k + 8], src[:, :]), inc="ve")
                for k, src in enumerate([g_m1n, g_m2n, g_d, g_s, g_r, g_w2]):
                    p.op("vector", lambda e, k=k, src=src: e.tensor_copy(
                        scr[:, 24 + k:25 + k], src[:, :]), inc="ve")
                p.op("sync", lambda e: e.dma_start(out=dbg["dGI"][:, 0:40],
                                                   in_=scr[:, 0:40]),
                     waits=[p.at("ve")], inc="gdb", dma=True)
        coef_done = p.at("ve")
        if debug:
            p.op("sync", lambda e: e.dma_start(out=dbg["dCOEF"][:, :, :],
                                               in_=coef[:, :, :]),
                 waits=[coef_done], inc="gdb", dma=True)

        p.op("sync", lambda e: e.dma_start(
            out=Y.rearrange("(c p) d -> p c d", p=128), in_=x_sb[:, :, :]),
            waits=[x_attn, h2_done], inc="gxs", dma=True)
        x_spill2 = p.at("gxs")

        w1_marks = [None] * 4
        m1_evac = [None] * 4
        m2_marks = [None, None, None]
        ff_marks = [None, None]
        acc_region = {}
        w2_used_te = None
        hs_used_te = None
        b2_used_te = None

        def m2slot(i):
            j = i % 3
            if j == 0:
                return ps_b[:, 0:512]
            if j == 1:
                return ps_b[:, 512:1024]
            return ps_c[:, 0:512]

        for e_i in range(NE):
            b2w = [x_spill2] if e_i == 0 else [b2_used_te]
            p.op("gpsimd", lambda e, e_i=e_i: e.dma_start(
                out=b2row[:, :], in_=B2I[e_i:e_i + 1, :]),
                waits=b2w, inc="gb2", dma=True)
            b2_ld = p.at("gb2")
            w2_chunks = []
            for fb in range(8):
                w2_chunks.append((fb,))
            w2_ld_done = None
            w2q = list(w2_chunks)

            def emit_w2_chunk():
                fb, = w2q.pop(0)
                w = []
                if e_i > 0 and w2_used_te is not None:
                    w.append(w2_used_te)
                if e_i == 0:
                    w.append(coef_done)
                p.op("gpsimd", lambda e, e_i=e_i, fb=fb: e.dma_start(
                    out=w2sb[:, 4 * fb:4 * fb + 4, :],
                    in_=W2I[e_i, 512 * fb:512 * fb + 512, :].rearrange(
                        "(c p) d -> p c d", p=128)),
                    waits=w, inc="gw2", dma=True)

            for tc in range(4):
                for ft in range(32):
                    slot = ft % 4
                    w = []
                    if w1_marks[slot] is not None:
                        w.append(w1_marks[slot])
                    if e_i == 0 and tc == 0 and ft == 0:
                        w.append(x_spill2)
                    w1sem = ["gwa", "gwb", "gwc", "gwd"][slot]
                    p.op("gpsimd", lambda e, e_i=e_i, ft=ft, slot=slot: e.dma_start(
                        out=w1buf[:, slot, :, :],
                        in_=W1TI[e_i, ft].rearrange("dt p f -> p dt f")),
                        waits=w, inc=w1sem, dma=True)
                    w1_ld = p.at(w1sem)
                    if tc == 0 and ft % 4 == 0 and w2q:
                        emit_w2_chunk()
                    bank = ft % 4
                    mw = [w1_ld, h2_done]
                    if m1_evac[bank] is not None:
                        mw.append(m1_evac[bank])
                    if hs_used_te is not None:
                        mw.append(hs_used_te)
                    for dt in range(8):
                        p.op("tensor", lambda e, ft=ft, dt=dt, tc=tc, slot=slot,
                             bank=bank: e.matmul(
                            ps_a[:, 512 * bank:512 * bank + 512],
                            w1buf[:, slot, dt, :],
                            h2T[:, dt, 512 * tc:512 * tc + 512],
                            start=(dt == 0), stop=(dt == 7)),
                            waits=mw if dt == 0 else (), inc="te" if dt == 7 else None)
                    w1_marks[slot] = p.at("te")
                    mmd = p.at("te")
                    p.op("vector", lambda e, ft=ft, bank=bank, e_i=e_i: e.tensor_scalar(
                        hslab[:, ft, :], ps_a[:, 512 * bank:512 * bank + 512],
                        b1pp[:, e_i, ft:ft + 1], zero_col[:, 0:1], ALU.add, ALU.max),
                        waits=[mmd, b1_ld], inc="ve")
                    m1_evac[bank] = p.at("ve")
                m1_done = p.at("ve")
                while tc == 0 and w2q:
                    emit_w2_chunk()
                if tc == 0:
                    w2_ld_done = p.at("gw2")
                for g in range(8):
                    ttl = g // 2
                    dc = g % 2
                    ttg = 4 * tc + ttl
                    si = g % 3
                    w = [m1_done, b2_ld, w2_ld_done, coef_done]
                    if m2_marks[si] is not None:
                        w.append(m2_marks[si])
                    p.op("tensor", lambda e, dc=dc, si=si: e.matmul(
                        m2slot(si), ones_row_bf[0:1, :],
                        b2row[0:1, 512 * dc:512 * dc + 512],
                        start=True, stop=False), waits=w, inc=None)
                    for ft in range(32):
                        p.op("tensor", lambda e, ft=ft, ttl=ttl, dc=dc, si=si: e.matmul(
                            m2slot(si), hslab[:, ft, 128 * ttl:128 * ttl + 128],
                            w2sb[:, ft, 512 * dc:512 * dc + 512],
                            start=False, stop=(ft == 31)),
                            inc="te" if ft == 31 else None)
                    mmd = p.at("te")
                    fw = [mmd]
                    if ff_marks[g % 2] is not None:
                        fw.append(ff_marks[g % 2])
                    p.op("vector", lambda e, g=g, ttg=ttg, e_i=e_i, si=si: e.tensor_scalar(
                        ffslot[:, g % 2, :], m2slot(si), coef[:, ttg, e_i:e_i + 1],
                        None, ALU.mult), waits=fw, inc="ve")
                    m2_marks[si] = p.at("ve")
                    gasem = "ga0" if g % 2 == 0 else "ga1"
                    accw = [m2_marks[si]]
                    if (ttg, dc) in acc_region:
                        accw.append(acc_region[(ttg, dc)])
                    p.op("gpsimd", lambda e, g=g, ttg=ttg, dc=dc, e_i=e_i: e.dma_start(
                        out=ACC[128 * ttg:128 * ttg + 128, 512 * dc:512 * dc + 512],
                        in_=ffslot[:, g % 2, :],
                        **({"accum_op": ALU.add} if e_i > 0 else {})),
                        waits=accw, inc=gasem, dma=True)
                    ff_marks[g % 2] = p.at(gasem)
                    acc_region[(ttg, dc)] = p.at(gasem)
                hs_used_te = p.at("te")
                if debug and e_i == 0 and tc == 0:
                    p.op("sync", lambda e: e.dma_start(out=dbg["dHS"][:, :, :],
                                                       in_=hslab[:, :, :]),
                         waits=[m1_done], inc="gdb", dma=True)
                    p.op("sync", lambda e: e.dma_start(out=dbg["dB1"][:, :, :],
                                                       in_=b1pp[:, :, :]),
                         inc="gdb", dma=True)
            w2_used_te = p.at("te")
            b2_used_te = p.at("te")
        acc_done0 = p.at("ga0")
        acc_done1 = p.at("ga1")

        fcw = [None, None]
        fc_ve = [None, None]
        for tt in range(16):
            sl = tt % 2
            w = [acc_done0, acc_done1, x_spill2]
            if fcw[sl] is not None:
                w.append(fcw[sl])
            if fc_ve[sl] is not None:
                w.append(fc_ve[sl])
            p.op("sync", lambda e, tt=tt, sl=sl: e.dma_start(
                out=fc_x[:, sl, :], in_=Y[128 * tt:128 * tt + 128, :]),
                waits=w, inc="gfl", dma=True)
            p.op("sync", lambda e, tt=tt, sl=sl: e.dma_start(
                out=fc_a[:, sl, :], in_=ACC[128 * tt:128 * tt + 128, :]),
                inc="gfl", dma=True)
            ld = p.at("gfl")
            p.op("vector", lambda e, sl=sl: e.tensor_tensor(
                fc_a[:, sl, :], fc_a[:, sl, :], gb_bc[:, :], ALU.mult),
                waits=[ld, bc_ready], inc="ve")
            p.op("vector", lambda e, sl=sl: e.tensor_tensor(
                fc_o[:, sl, :], fc_a[:, sl, :], fc_x[:, sl, :], ALU.add), inc="ve")
            od = p.at("ve")
            fssem = "gfs0" if sl == 0 else "gfs1"
            p.op("sync", lambda e, tt=tt, sl=sl: e.dma_start(
                out=Y[128 * tt:128 * tt + 128, :], in_=fc_o[:, sl, :]),
                waits=[od], inc=fssem, dma=True)
            fcw[sl] = p.at(fssem)
            fc_ve[sl] = od
        finals = [("gfs0", p.cnt["gfs0"]), ("gfs1", p.cnt["gfs1"])]
        if debug:
            finals.append(("gdb", p.cnt["gdb"]))
        p.op("sync", lambda e: e.nop(), waits=finals, inc=None)

        with nc.allow_low_precision(reason="bf16 operands are deliberate; accumulations stay fp32"):
            p.emit(block, limit=limit)
        build_nc.total_ops = p.order
    return nc


_NC_CACHE = {}


def _get_nc(debug=False):
    if debug not in _NC_CACHE:
        _NC_CACHE[debug] = build_nc(debug)
    return _NC_CACHE[debug]


def make_in_maps(inputs):
    x = np.asarray(inputs["L0_attn_x"])
    wq = np.ascontiguousarray(inputs["L0_attn_Wq"], dtype=np.float32)
    wk = np.ascontiguousarray(inputs["L0_attn_Wk"], dtype=np.float32)
    wv = np.ascontiguousarray(inputs["L0_attn_Wv"], dtype=np.float32)
    wo = np.ascontiguousarray(inputs["L0_attn_Wo"], dtype=np.float32)
    bq = np.asarray(inputs["L0_attn_bq"], dtype=np.float32)
    bk = np.asarray(inputs["L0_attn_bk"], dtype=np.float32)
    bv = np.asarray(inputs["L0_attn_bv"], dtype=np.float32)
    bo = np.asarray(inputs["L0_attn_bo"], dtype=np.float32)
    rel = np.asarray(inputs["L0_attn_rel_emb"], dtype=np.float32)
    wg = np.asarray(inputs["L0_ffn_Wg"], dtype=np.float32)
    w1 = np.asarray(inputs["L0_ffn_W1"], dtype=np.float32)
    b1 = np.asarray(inputs["L0_ffn_b1"], dtype=np.float32)
    w2 = np.asarray(inputs["L0_ffn_W2"], dtype=np.float32)
    b2 = np.asarray(inputs["L0_ffn_b2"], dtype=np.float32)
    c = np.asarray(inputs["L0_c"], dtype=np.float32)
    wmod = np.ascontiguousarray(inputs["L0_wmod"], dtype=np.float32)

    shared = {
        "WQ": wq, "WK": wk, "WV": wv, "WO": wo,
        "BQP": np.ascontiguousarray(bq.reshape(8, 128).T),
        "BKP": np.ascontiguousarray(bk.reshape(8, 128).T),
        "BVROW": np.ascontiguousarray(bv[None, :]),
        "BOROW": np.ascontiguousarray(bo[None, :]),
        "GB": _gather_bias_tiles(rel),
        "REL31": np.ascontiguousarray(rel[31:32, :]),
        "WGP": np.ascontiguousarray(wg.reshape(8, 128, NE).transpose(1, 0, 2)),
        "W1TI": np.ascontiguousarray(
            w1.reshape(NE, 8, 128, 32, 128).transpose(0, 3, 1, 2, 4)),
        "B1P": np.ascontiguousarray(b1.reshape(NE, 32, 128).transpose(2, 0, 1)),
        "W2I": np.ascontiguousarray(w2),
        "B2I": np.ascontiguousarray(b2),
        "WMOD": wmod,
    }
    in_maps = []
    for b in range(8):
        m = dict(shared)
        m["X"] = np.ascontiguousarray(x[b], dtype=np.float32)
        m["CP"] = np.ascontiguousarray(c[b].reshape(8, 128).T)
        in_maps.append(m)
    return in_maps


def kernel(**inputs):
    nc = _get_nc(False)
    in_maps = make_in_maps(inputs)
    res = run_bass_kernel_spmd(nc, in_maps, core_ids=list(range(8)))
    out = np.stack([np.asarray(r["Y"]) for r in res.results], axis=0)
    return out.astype(np.float32)


# version 03 (final)
# speedup vs baseline: 1.0315x; 1.0054x over previous
import math
import numpy as np
import ml_dtypes
import concourse.bass as bass
import concourse.mybir as mybir
from concourse.bass_utils import run_bass_kernel_spmd

F32 = mybir.dt.float32
BF16 = mybir.dt.bfloat16
AF = mybir.ActivationFunctionType
ALU = mybir.AluOpType
AX = mybir.AxisListType

T = 2048
D = 1024
NH = 16
FF = 4096
NE = 8
EPS = 1e-5

ENGINES = ("tensor", "vector", "scalar", "gpsimd", "sync")


def _bucket_np(n):
    n = np.asarray(n)
    nf = np.maximum(n, 1).astype(np.float32) / np.float32(16.0)
    val = 16 + (np.log(nf) / np.float32(math.log(8.0)) * np.float32(16.0)).astype(np.int32)
    val = np.minimum(val, 31)
    return np.where(n < 16, n, val).astype(np.int64)


def _gather_bias_tiles(rel_emb):
    a = np.arange(128)
    qp, jj = a[:, None], a[None, :]
    idx = np.stack([
        _bucket_np(np.maximum(qp - jj, 0)),
        _bucket_np(qp - jj + 128),
        _bucket_np(np.maximum(jj - qp, 0)),
        _bucket_np(jj - qp + 128),
    ])
    g = np.asarray(rel_emb)[idx]
    return np.ascontiguousarray(g.transpose(3, 0, 1, 2)).astype(np.float32)


class Plan:
    def __init__(self, nc, sems):
        self.nc = nc
        self.sems = sems
        self.cnt = {k: 0 for k in sems}
        self.ops = {e: [] for e in ENGINES}
        self.order = 0

    def at(self, sem):
        return (sem, self.cnt[sem])

    ESEM = {"vector": "ve", "scalar": "se", "gpsimd": "ge"}

    def op(self, engine, fn, waits=(), inc=None, dma=False):
        waits = list(waits)
        if engine in self.ESEM and not dma:
            es = self.ESEM[engine]
            waits.append((es, self.cnt[es]))
            if inc is None:
                inc = es
        for s, v in waits:
            assert v <= self.cnt[s], f"wait on future {s}:{v} > {self.cnt[s]}"
        incs = None
        if inc is not None:
            amt = 16 if dma else 1
            self.cnt[inc] += amt
            incs = (inc, amt)
        import inspect as _insp
        ln = _insp.currentframe().f_back.f_lineno
        self.meta = getattr(self, "meta", [])
        self.meta.append((self.order, engine, ln))
        self.ops[engine].append((fn, tuple(waits), incs, self.order))
        self.order += 1

    def emit(self, block, limit=None):
        sems, ops = self.sems, self.ops

        def runner(name):
            def _run(eng):
                last = {}
                for fn, waits, incs, order in ops[name]:
                    if limit is not None and order >= limit:
                        continue
                    for s, v in waits:
                        if v > 0 and last.get(s, -1) < v:
                            eng.wait_ge(sems[s], v)
                            last[s] = v
                    instr = fn(eng)
                    if incs is not None:
                        instr.then_inc(sems[incs[0]], incs[1])
            return _run

        block.tensor(runner("tensor"))
        block.vector(runner("vector"))
        block.scalar(runner("scalar"))
        block.gpsimd(runner("gpsimd"))
        block.sync(runner("sync"))


def build_nc(debug=False, limit=None):
    from contextlib import ExitStack
    nc = bass.Bass("TRN2", target_bir_lowering=False, detect_race_conditions=False)

    def din(name, shape, dtype=F32):
        return nc.dram_tensor(name, shape, dtype, kind="ExternalInput").ap()

    X = din("X", [T, D])
    WQ = din("WQ", [D, D]); WK = din("WK", [D, D])
    WV = din("WV", [D, D]); WO = din("WO", [D, D])
    BQP = din("BQP", [128, 8]); BKP = din("BKP", [128, 8])
    BVROW = din("BVROW", [1, D]); BOROW = din("BOROW", [1, D])
    GB = din("GB", [NH, 4, 128, 128])
    REL31 = din("REL31", [1, NH])
    WGP = din("WGP", [128, 8, NE])
    W1TI = din("W1TI", [NE, 32, 8, 128, 128])
    B1P = din("B1P", [128, NE, 32])
    W2I = din("W2I", [NE, FF, D])
    B2I = din("B2I", [NE, D])
    CP = din("CP", [128, 8])
    WMOD = din("WMOD", [D, 6 * D])
    Y = nc.dram_tensor("Y", [T, D], F32, kind="ExternalOutput").ap()
    ACC = nc.dram_tensor("ACC", [T, D], F32, kind="Internal").ap()

    dbg = {}
    if debug:
        for name, shape, dt in [
            ("dMOD", [1, 6 * D], F32), ("dHT", [128, 8, T], BF16),
            ("dQT", [128, 8, T], BF16), ("dKT", [128, 8, T], BF16),
            ("dV", [128, 16, D], BF16), ("dMALL", [128, NH, 16], F32),
            ("dOT", [128, 8, T], BF16), ("dXA", [128, 16, D], F32),
            ("dH2T", [128, 8, T], BF16), ("dCOEF", [128, 16, NE], F32),
            ("dHS", [128, 32, 512], BF16), ("dGC", [128, 2, 4, 128], BF16),
            ("dGI", [128, 40], F32), ("dWG", [128, 8, NE], BF16), ("dB1", [128, NE, 32], F32),
        ]:
            dbg[name] = nc.dram_tensor(name, shape, dt, kind="ExternalOutput").ap()

    with ExitStack() as stack:
        ARENA = 212480
        backing = stack.enter_context(
            nc.sbuf_tensor("arena", [128, ARENA], mybir.dt.uint8))
        base_addr = nc.lookup_mloc(backing).addr

        def place(name, shape, dtype, off):
            per = 1
            for s in shape[1:]:
                per *= s
            per *= mybir.dt.size(dtype)
            assert off + per <= ARENA, f"{name} overflows arena: {off}+{per}"
            h = nc.virtual_sbuf_tensor(name, shape, dtype)
            nc._tensor(h.name, h.shape, h.dtype, type="SB")
            mloc = nc.lookup_mloc(h)
            mloc.allocated = True
            mloc.addr = base_addr + off
            return h

        x_sb = place("x_sb", [128, 16, D], F32, 0)
        stage = place("stage", [128, 4, D], BF16, 0)
        qT = place("qT", [128, 8, T], BF16, 0)
        kT = place("kT", [128, 8, T], BF16, 32768)
        hslab = place("hslab", [128, 32, 512], BF16, 0)
        fc_x = place("fc_x", [128, 2, D], F32, 32768)
        fc_a = place("fc_a", [128, 2, D], F32, 40960)
        fc_o = place("fc_o", [128, 2, D], F32, 49152)
        mod_row = place("mod_row", [1, 6 * D], F32, 65536)
        ca1_bc = place("ca1_bc", [128, D], F32, 90112)
        sa_bc = place("sa_bc", [128, D], F32, 94208)
        v_sb = place("v_sb", [128, 16, D], BF16, 65536)
        stage_v = place("stage_v", [128, 4, D], BF16, 65536)
        h2T = place("h2T", [128, 8, T], BF16, 65536)
        hT = place("hT", [128, 8, T], BF16, 98304)
        oT = place("oT", [128, 8, T], BF16, 98304)
        w2sb = place("w2sb", [128, 32, D], BF16, 98304)
        wqT = place("wqT", [128, 8, D], BF16, 131072)
        wvT = place("wvT", [128, 8, D], BF16, 131072)
        bo_f = place("bo_f", [1, D], F32, 131072)
        wkT = place("wkT", [128, 8, D], BF16, 147456)
        wmod_buf = place("wmod_buf", [128, 2, T], F32, 147456)
        ptile = place("ptile", [128, 4, 512], BF16, 131072)
        graw = place("graw", [128, 2, 4, 128], F32, 135168)
        gcomb = place("gcomb", [128, 2, 4, 128], BF16, 139264)
        m_all = place("m_all", [128, 16], F32, 141312)
        mT16 = place("mT16", [16, 128], BF16, 141440)
        m_row = place("m_row", [1, 2, T], BF16, 141824)
        r_bf = place("r_bf", [128, 2, 512], BF16, 150016)
        rbc_sb = place("rbc_sb", [128, 512], F32, 152064)
        pexp_scr = place("pexp_scr", [128, 1024], F32, 154112)
        pexp2 = place("pexp2", [128, 1024], F32, 158208)
        cb1_bc = place("cb1_bc", [128, D], F32, 154112)
        sb_bc = place("sb_bc", [128, D], F32, 158208)
        woT = place("woT", [128, 8, D], BF16, 163840)
        w1buf = place("w1buf", [128, 6, 8, 128], BF16, 163840)
        ffslot = place("ffslot", [128, 2, 512], F32, 176128)
        iden_bf = place("iden_bf", [128, 128], BF16, 180224)
        iden_f = place("iden_f", [128, 128], F32, 180480)
        mask1 = place("mask1", [128, 128], BF16, 180992)
        mask1T = place("mask1T", [128, 128], BF16, 181248)
        ones_row_bf = place("ones_row_bf", [128, 128], BF16, 181504)
        ones_row_f = place("ones_row_f", [1, 128], F32, 181760)
        ones_col_bf = place("ones_col_bf", [128, 1], BF16, 182272)
        iota_s = place("iota_s", [128, 128], F32, 182336)
        scr = place("scr", [128, D], F32, 182848)
        h_tile = place("h_tile", [128, D], BF16, 186944)
        stats6 = place("stats6", [128, 12], F32, 188992)
        mv2 = place("mv2", [128, 2], F32, 189056)
        rstd = place("rstd", [128, 1], F32, 189120)
        rr = place("rr", [128, 1], F32, 189124)
        g_eq1 = place("g_eq1", [128, NE], F32, 189184)
        g_eq2 = place("g_eq2", [128, NE], F32, 189216)
        g_msk = place("g_msk", [128, NE], F32, 189248)
        g_m1n = place("g_m1n", [128, 1], F32, 189280)
        g_m2n = place("g_m2n", [128, 1], F32, 189284)
        g_d = place("g_d", [128, 1], F32, 189288)
        g_s = place("g_s", [128, 1], F32, 189292)
        g_r = place("g_r", [128, 1], F32, 189296)
        g_w2 = place("g_w2", [128, 1], F32, 189300)
        coef = place("coef", [128, 16, NE], F32, 189312)
        gb_bc = place("gb_bc", [128, D], F32, 189824)
        cb_row = place("cb_row", [1, D], F32, 193920)
        sb_row = place("sb_row", [1, D], F32, 198016)
        bv_row = place("bv_row", [1, D], BF16, 202112)
        gabo_row = place("gabo_row", [1, D], BF16, 204160)
        bq_pp = place("bq_pp", [128, 8], F32, 206208)
        bk_pp = place("bk_pp", [128, 8], F32, 206240)
        nr31 = place("nr31", [128, NH], F32, 206272)
        rel31row = place("rel31row", [1, NH], F32, 206336)
        c_col = place("c_col", [128, 8], BF16, 206400)
        c_sil = place("c_sil", [128, 8], F32, 206760)
        c_col32 = place("c_col32", [128, 8], F32, 206792)
        b2row = place("b2row", [1, D], BF16, 206824)
        b1pp = place("b1pp", [128, NE, 32], F32, 208872)
        wg_sb = place("wg_sb", [128, 8, NE], BF16, 206432)
        mod_pp = place("mod_pp", [128, 48], F32, 206560)
        zero_col = place("zero_col", [128, 1], F32, 206756)
        eps_col = place("eps_col", [128, 1], F32, 206752)

        ps_a = stack.enter_context(nc.psum_tensor("ps_a", [128, 2048], F32))
        ps_b = stack.enter_context(nc.psum_tensor("ps_b", [128, 1024], F32))
        ps_c = stack.enter_context(nc.psum_tensor("ps_c", [128, 512], F32))
        ps_t = stack.enter_context(nc.psum_tensor("ps_t", [128, 1024], BF16))

        semnames = ["te", "ve", "se", "ge", "gx", "gsm", "gw0", "gw1", "gst",
                    "gg0", "gg1", "gm0", "gm1", "gwa", "gwb", "gwc", "gwd", "gwe", "gwf", "gsc",
                    "gw2", "gb2", "ga0", "ga1", "gfl", "gfs0", "gfs1", "gxs",
                    "gdb"]
        sems = {n: stack.enter_context(nc.semaphore("s_" + n))
                for n in semnames}
        block = stack.enter_context(nc.Block())
        p = Plan(nc, sems)

        p.op("sync", lambda e: e.dma_start(
            out=x_sb[:, :, :], in_=X.rearrange("(c p) d -> p c d", p=128)),
            inc="gx", dma=True)
        x_loaded = p.at("gx")
        p.op("sync", lambda e: e.dma_start(out=bq_pp[:, :], in_=BQP[:, :]), inc="gsm", dma=True)
        p.op("sync", lambda e: e.dma_start(out=bk_pp[:, :], in_=BKP[:, :]), inc="gsm", dma=True)
        p.op("sync", lambda e: e.dma_start(out=rel31row[:, :], in_=REL31[:, :]), inc="gsm", dma=True)
        p.op("sync", lambda e: e.dma_start(out=bo_f[:, :], in_=BOROW[:, :]), inc="gsm", dma=True)

        p.op("sync", lambda e: e.dma_start(out=c_col32[:, :], in_=CP[:, :]), inc="gsm", dma=True)
        p.op("gpsimd", lambda e: e.dma_start(out=wg_sb[:, :, :], in_=WGP[:, :, :]), inc="gsc", dma=True)
        p.op("gpsimd", lambda e: e.dma_start(out=bv_row[:, :], in_=BVROW[:, :]), inc="gsc", dma=True)
        small_ld = p.at("gsm")
        cast_small = p.at("gsc")


        p.op("vector", lambda e: e.memset(ones_row_bf[:, :], 1.0), inc="ve")
        p.op("vector", lambda e: e.memset(ones_row_f[:, :], 1.0), inc="ve")
        p.op("vector", lambda e: e.memset(ones_col_bf[:, :], 1.0), inc="ve")
        p.op("vector", lambda e: e.memset(eps_col[:, :], EPS), inc="ve")
        p.op("vector", lambda e: e.memset(zero_col[:, :], 0.0), inc="ve")
        p.op("vector", lambda e: e.memset(ps_a[:, :], 0.0), inc="ve")
        p.op("vector", lambda e: e.memset(ps_b[:, :], 0.0), inc="ve")
        p.op("vector", lambda e: e.memset(ps_c[:, :], 0.0), inc="ve")
        psz = p.at("ve")
        p.op("gpsimd", lambda e: e.iota(iota_s[:, :], [[0, 1], [0, 1], [0, 1], [-1, 128]],
                                        base=0, channel_multiplier=1,
                                        allow_small_or_imprecise_dtypes=True), inc="ge")
        io1 = p.at("ge")
        p.op("vector", lambda e: e.tensor_scalar(iden_bf[:, :], iota_s[:, :], 0.0, None,
                                                 ALU.is_equal), waits=[io1], inc="ve")
        p.op("vector", lambda e: e.tensor_scalar(iden_f[:, :], iota_s[:, :], 0.0, None,
                                                 ALU.is_equal), waits=[io1], inc="ve")
        p.op("vector", lambda e: e.tensor_scalar(mask1[:, :], iota_s[:, :], 0.0, -30000.0,
                                                 ALU.is_lt, ALU.mult), waits=[io1], inc="ve")
        p.op("vector", lambda e: e.tensor_scalar(mask1T[:, :], iota_s[:, :], 0.0, -30000.0,
                                                 ALU.is_gt, ALU.mult), waits=[io1], inc="ve")
        consts_ready = p.at("ve")

        p.op("scalar", lambda e: e.activation(c_sil[:, :], c_col32[:, :], AF.Silu),
             waits=[small_ld, cast_small], inc="se")
        csil_ready = p.at("se")

        for ntb in range(3):
            for kt in range(8):
                slot = kt % 2
                w_wait = [p.at("te")] if (ntb or kt >= 2) else []
                p.op("sync", lambda e, ntb=ntb, kt=kt, slot=slot: e.dma_start(
                    out=wmod_buf[:, slot, :],
                    in_=WMOD[128 * kt:128 * kt + 128, 2048 * ntb:2048 * ntb + 2048]),
                    waits=w_wait, inc="gw0" if slot == 0 else "gw1", dma=True)
                ld = p.at("gw0" if slot == 0 else "gw1")
                for nt in range(4):
                    p.op("tensor", lambda e, kt=kt, slot=slot, nt=nt: e.matmul(
                        ps_a[0:1, 512 * nt:512 * nt + 512],
                        c_sil[:, kt:kt + 1], wmod_buf[:, slot, 512 * nt:512 * nt + 512],
                        start=(kt == 0), stop=(kt == 7)),
                        waits=[ld, csil_ready, psz] + ([p.at("ve")] if (ntb and kt == 0 and nt == 0) else []),
                        inc="te" if nt == 3 else None)
            mm_done = p.at("te")
            p.op("vector", lambda e, ntb=ntb: e.tensor_copy(
                mod_row[0:1, 2048 * ntb:2048 * ntb + 2048], ps_a[0:1, 0:2048]),
                waits=[mm_done], inc="ve")
        mod_raw = p.at("ve")

        for off in (2 * D, 5 * D):
            p.op("vector", lambda e, off=off: e.tensor_scalar(
                scr[0:1, 0:D], mod_row[0:1, off:off + D], 0.0, 2.0, ALU.is_ge, ALU.mult),
                waits=[mod_raw], inc="ve")
            p.op("vector", lambda e: e.tensor_scalar(
                scr[0:1, 0:D], scr[0:1, 0:D], -1.0, None, ALU.add), inc="ve")
            p.op("vector", lambda e, off=off: e.tensor_scalar(
                ca1_bc[0:1, 0:D], mod_row[0:1, off:off + D], -1.0, None, ALU.mult),
                inc="ve")
            p.op("vector", lambda e, off=off: e.tensor_tensor(
                mod_row[0:1, off:off + D], mod_row[0:1, off:off + D], ca1_bc[0:1, 0:D],
                ALU.max), inc="ve")
            p.op("vector", lambda e, off=off: e.tensor_scalar(
                mod_row[0:1, off:off + D], mod_row[0:1, off:off + D], 0.25, None,
                ALU.max), inc="ve")
            p.op("vector", lambda e, off=off: e.tensor_tensor(
                mod_row[0:1, off:off + D], mod_row[0:1, off:off + D], scr[0:1, 0:D],
                ALU.mult), inc="ve")
        mod_ready = p.at("ve")
        if debug:
            p.op("sync", lambda e: e.dma_start(out=dbg["dMOD"][:, :], in_=mod_row[:, :]),
                 waits=[mod_ready], inc="gdb", dma=True)

        for c6 in range(48):
            p.op("tensor", lambda e, c6=c6: e.transpose(
                ps_c[:, c6:c6 + 1], mod_row[0:1, 128 * c6:128 * c6 + 128],
                iden_f[0:1, 0:1]),
                waits=[mod_ready, consts_ready], inc="te" if c6 == 47 else None)
        p.op("vector", lambda e: e.tensor_copy(mod_pp[:, :], ps_c[:, 0:48]),
             waits=[p.at("te")], inc="ve")
        modpp_ready = p.at("ve")

        p.op("tensor", lambda e: e.matmul(ps_b[:, 0:NH], ones_row_f[:, :], rel31row[:, :],
                                          start=True, stop=True),
             waits=[small_ld, consts_ready, modpp_ready], inc="te")
        p.op("vector", lambda e: e.tensor_scalar(nr31[:, :], ps_b[:, 0:NH], -0.125, None,
                                                 ALU.mult), waits=[p.at("te")], inc="ve")
        nr31_ready = p.at("ve")

        def bc_build(dst, src_off, add1, wait_extra):
            for c in range(2):
                p.op("tensor", lambda e, src_off=src_off, c=c: e.matmul(
                    ps_b[:, 512 * c:512 * c + 512], ones_row_f[:, :],
                    mod_row[0:1, src_off + 512 * c:src_off + 512 * c + 512],
                    start=True, stop=True),
                    waits=[mod_ready] + wait_extra, inc="te" if c == 1 else None)
            mmd = p.at("te")
            for c in range(2):
                if add1:
                    p.op("vector", lambda e, dst=dst, c=c: e.tensor_scalar(
                        dst[:, 512 * c:512 * c + 512], ps_b[:, 512 * c:512 * c + 512],
                        1.0, None, ALU.add), waits=[mmd], inc="ve")
                else:
                    p.op("vector", lambda e, dst=dst, c=c: e.tensor_copy(
                        dst[:, 512 * c:512 * c + 512], ps_b[:, 512 * c:512 * c + 512]),
                        waits=[mmd], inc="ve")

        bc_build(ca1_bc, 1 * D, True, [nr31_ready])
        bc_build(sa_bc, 0 * D, False, [p.at("ve")])
        bc_build(gb_bc, 5 * D, False, [p.at("ve")])
        p.op("vector", lambda e: e.tensor_copy(cb_row[:, :], mod_row[0:1, 4 * D:5 * D]),
             waits=[mod_ready], inc="ve")
        p.op("vector", lambda e: e.tensor_copy(sb_row[:, :], mod_row[0:1, 3 * D:4 * D]),
             waits=[mod_ready], inc="ve")
        p.op("vector", lambda e: e.tensor_tensor(
            gabo_row[:, :], bo_f[0:1, :], mod_row[0:1, 2 * D:3 * D], ALU.mult),
            waits=[mod_ready, small_ld], inc="ve")
        bc_ready = p.at("ve")

        def ln_block(dst_T, c1bc, sbc, waits0):
            for tt in range(16):
                w = list(waits0) if tt == 0 else [p.at("te")]
                p.op("vector", lambda e, tt=tt: e.bn_stats(
                    stats6[:, 0:6], x_sb[:, tt, 0:512]), waits=w + [x_loaded], inc="ve")
                p.op("vector", lambda e, tt=tt: e.bn_stats(
                    stats6[:, 6:12], x_sb[:, tt, 512:1024]), inc="ve")
                p.op("vector", lambda e: e.bn_aggr(mv2[:, :], stats6[:, :]),
                     waits=[p.at("ve")], inc="ve")
                sd = p.at("ve")
                p.op("scalar", lambda e: e.activation(rstd[:, :], mv2[:, 1:2], AF.Sqrt,
                                                      bias=eps_col[:, 0:1]), waits=[sd], inc="se")
                p.op("vector", lambda e: e.reciprocal(rr[:, :], rstd[:, :]),
                     waits=[p.at("se")], inc="ve")
                p.op("vector", lambda e, tt=tt: e.tensor_scalar(
                    scr[:, :], x_sb[:, tt, :], mv2[:, 0:1], rr[:, 0:1],
                    ALU.subtract, ALU.mult), waits=[p.at("ve")], inc="ve")
                p.op("vector", lambda e, c1bc=c1bc: e.tensor_tensor(
                    scr[:, :], scr[:, :], c1bc[:, :], ALU.mult), inc="ve")
                p.op("vector", lambda e, sbc=sbc: e.tensor_tensor(
                    h_tile[:, :], scr[:, :], sbc[:, :], ALU.add), inc="ve")
                hready = p.at("ve")
                for dt in range(8):
                    p.op("tensor", lambda e, dt=dt: e.transpose(
                        ps_t[:, 128 * dt:128 * dt + 128],
                        h_tile[:, 128 * dt:128 * dt + 128], iden_bf[:, :]),
                        waits=[hready], inc="te" if dt == 7 else None)
                p.op("vector", lambda e, tt=tt, dst_T=dst_T: e.tensor_copy(
                    dst_T[:, :, 128 * tt:128 * tt + 128], ps_t[:, :]),
                    waits=[p.at("te")], inc="ve")

        ln_block(hT, ca1_bc, sa_bc, [bc_ready, consts_ready])
        h_done = p.at("ve")
        if debug:
            p.op("sync", lambda e: e.dma_start(out=dbg["dHT"][:, :, :], in_=hT[:, :, :]),
                 waits=[h_done], inc="gdb", dma=True)
        p.op("sync", lambda e: e.dma_start(
            out=Y.rearrange("(c p) d -> p c d", p=128), in_=x_sb[:, :, :]),
            waits=[h_done], inc="gxs", dma=True)
        x_spilled = p.at("gxs")

        def w_prep(dst, WSRC, stage_t, scale_ga, waits0):
            for half in range(2):
                w = list(waits0) + ([p.at("ve")] if half else [])
                p.op("gpsimd", lambda e, WSRC=WSRC, half=half, stage_t=stage_t: e.dma_start(
                    out=stage_t[:, :, :],
                    in_=WSRC[512 * half:512 * half + 512, :].rearrange(
                        "(c p) d -> p c d", p=128)),
                    waits=w, inc="gst", dma=True)
                ld = p.at("gst")
                for c in range(4):
                    ot = 4 * half + c
                    if scale_ga:
                        p.op("vector", lambda e, c=c, ot=ot, stage_t=stage_t: e.tensor_scalar(
                            stage_t[:, c, :], stage_t[:, c, :],
                            mod_pp[:, 16 + ot:17 + ot], None, ALU.mult),
                            waits=[ld, modpp_ready], inc="ve")
                        ld2 = [p.at("ve")]
                    else:
                        ld2 = [ld]
                    for it in range(8):
                        p.op("tensor", lambda e, c=c, it=it, stage_t=stage_t: e.transpose(
                            ps_t[:, 128 * it:128 * it + 128],
                            stage_t[:, c, 128 * it:128 * it + 128], iden_bf[:, :]),
                            waits=ld2 + ([p.at("ve")] if (it == 0 and (c or half)) else []),
                            inc="te" if it == 7 else None)
                    p.op("vector", lambda e, ot=ot, dst=dst: e.tensor_copy(
                        dst[:, :, 128 * ot:128 * ot + 128], ps_t[:, :]),
                        waits=[p.at("te")], inc="ve")

        w_prep(wqT, WQ, stage, False, [x_spilled, consts_ready])
        wq_ready = p.at("ve")
        w_prep(wkT, WK, stage, False, [p.at("ve")])
        wk_ready = p.at("ve")
        w_prep(woT, WO, stage, True, [p.at("ve")])
        wo_ready = p.at("ve")

        evac_marks = [None] * 4

        def proj_out_T(dst_T, wT, bias_pp, waits0):
            g = 0
            for ot in range(8):
                for tc in range(4):
                    bank = g % 4
                    w = list(waits0)
                    if evac_marks[bank] is not None:
                        w.append(evac_marks[bank])
                    for it in range(8):
                        p.op("tensor", lambda e, ot=ot, tc=tc, it=it, bank=bank, wT=wT: e.matmul(
                            ps_a[:, 512 * bank:512 * bank + 512],
                            wT[:, it, 128 * ot:128 * ot + 128],
                            hT[:, it, 512 * tc:512 * tc + 512],
                            start=(it == 0), stop=(it == 7)),
                            waits=w if it == 0 else (), inc="te" if it == 7 else None)
                    mmd = p.at("te")
                    p.op("vector", lambda e, ot=ot, tc=tc, bank=bank, dst_T=dst_T,
                         bias_pp=bias_pp: e.tensor_scalar(
                        dst_T[:, ot, 512 * tc:512 * tc + 512],
                        ps_a[:, 512 * bank:512 * bank + 512],
                        bias_pp[:, ot:ot + 1], None, ALU.add),
                        waits=[mmd], inc="ve")
                    evac_marks[bank] = p.at("ve")
                    g += 1

        proj_out_T(qT, wqT, bq_pp, [wq_ready, h_done, small_ld])
        w_prep(wvT, WV, stage_v, False, [p.at("ve")])
        wv_ready = p.at("ve")
        proj_out_T(kT, wkT, bk_pp, [wk_ready])

        for tt in range(16):
            for oc in range(2):
                bank = (tt * 2 + oc) % 4
                w = [wv_ready, cast_small]
                if evac_marks[bank] is not None:
                    w.append(evac_marks[bank])
                for it in range(8):
                    p.op("tensor", lambda e, tt=tt, oc=oc, it=it, bank=bank: e.matmul(
                        ps_a[:, 512 * bank:512 * bank + 512],
                        hT[:, it, 128 * tt:128 * tt + 128],
                        wvT[:, it, 512 * oc:512 * oc + 512],
                        start=(it == 0), stop=False),
                        waits=w if it == 0 else (), inc=None)
                p.op("tensor", lambda e, oc=oc, bank=bank: e.matmul(
                    ps_a[:, 512 * bank:512 * bank + 512],
                    ones_row_bf[0:1, :], bv_row[0:1, 512 * oc:512 * oc + 512],
                    start=False, stop=True), inc="te")
                mmd = p.at("te")
                p.op("vector", lambda e, tt=tt, oc=oc, bank=bank: e.tensor_copy(
                    v_sb[:, tt, 512 * oc:512 * oc + 512],
                    ps_a[:, 512 * bank:512 * bank + 512]),
                    waits=[mmd], inc="ve")
                evac_marks[bank] = p.at("ve")
        v_done = p.at("ve")
        if debug:
            p.op("sync", lambda e: e.dma_start(out=dbg["dQT"][:, :, :], in_=qT[:, :, :]),
                 waits=[v_done], inc="gdb", dma=True)
            p.op("sync", lambda e: e.dma_start(out=dbg["dKT"][:, :, :], in_=kT[:, :, :]),
                 waits=[v_done], inc="gdb", dma=True)
            p.op("sync", lambda e: e.dma_start(out=dbg["dV"][:, :, :], in_=v_sb[:, :, :]),
                 waits=[v_done], inc="gdb", dma=True)

        gc_used = [None, None]
        mrow_prev = None
        exp_prev2 = [("se", 0), ("se", 0)]
        gpair = 0
        graw_used = [None, None]
        mrow_used = [None, None]
        pass1_ps_free = None
        pvc_free = None
        psb_free = None

        for h in range(NH):
            hp = h % 2
            hb = 64 * (h % 2)
            hc = h // 2
            gw_w = [v_done] if h < 2 else [gc_used[hp]]
            p.op("sync", lambda e, h=h, hp=hp: e.dma_start(
                out=graw[:, hp, :, :],
                in_=GB[h].rearrange("k p f -> p k f")),
                waits=[w for w in gw_w if w], inc="gg0" if hp == 0 else "gg1",
                dma=True)
            gld = p.at("gg0" if hp == 0 else "gg1")
            for k in range(4):
                p.op("gpsimd", lambda e, hp=hp, k=k, h=h: e.tensor_scalar(
                    gcomb[:, hp, k, :], graw[:, hp, k, :], 0.125, nr31[:, h:h + 1],
                    ALU.mult, ALU.add), waits=[gld, nr31_ready], inc="ge")
            p.op("gpsimd", lambda e, hp=hp: e.tensor_tensor(
                gcomb[:, hp, 0, :], gcomb[:, hp, 0, :], mask1[:, :], ALU.add),
                waits=[consts_ready], inc="ge")
            p.op("gpsimd", lambda e, hp=hp: e.tensor_tensor(
                gcomb[:, hp, 2, :], gcomb[:, hp, 2, :], mask1T[:, :], ALU.add),
                inc="ge")
            gc_ready = p.at("ge")
            if debug and h == 0:
                p.op("sync", lambda e: e.dma_start(out=dbg["dGC"][:, :, :, :],
                                                   in_=gcomb[:, :, :, :]),
                     waits=[gc_ready], inc="gdb", dma=True)

            for qt in range(16):
                C = qt // 4 + 1
                w0 = [gc_ready, v_done]
                if pass1_ps_free is not None:
                    w0.append(pass1_ps_free)
                for jc in range(C):
                    W = 512 if jc < C - 1 else 128 * (qt % 4) + 128
                    p.op("tensor", lambda e, qt=qt, jc=jc, W=W, hb=hb, hc=hc: e.matmul(
                        ps_a[:, 512 * jc:512 * jc + W],
                        qT[hb:hb + 64, hc, 128 * qt:128 * qt + 128],
                        kT[hb:hb + 64, hc, 512 * jc:512 * jc + W],
                        start=True, stop=False, skip_group_check=True),
                        waits=w0 if jc == 0 else (), inc=None)
                p.op("tensor", lambda e, qt=qt, hp=hp: e.matmul(
                    ps_a[:, 128 * qt:128 * qt + 128], iden_bf[:, :],
                    gcomb[:, hp, 0, :], start=False, stop=(qt == 0),
                    skip_group_check=True), inc="te" if qt == 0 else None)
                if qt > 0:
                    p.op("tensor", lambda e, qt=qt, hp=hp: e.matmul(
                        ps_a[:, 128 * qt - 128:128 * qt], iden_bf[:, :],
                        gcomb[:, hp, 1, :], start=False, stop=True,
                        skip_group_check=True), inc="te")
                mmd = p.at("te")
                p.op("vector", lambda e, qt=qt: e.reduce_max(
                    m_all[:, qt:qt + 1], ps_a[:, 0:128 * qt + 128], axis=AX.X,
                    negate=True), waits=[mmd], inc="ve")
                pass1_ps_free = p.at("ve")
            p.op("tensor", lambda e: e.transpose(ps_c[0:16, 0:128], m_all[:, :],
                                                 iden_f[:, :]),
                 waits=[pass1_ps_free] + ([pvc_free] if pvc_free else []), inc="te")
            mtw = [p.at("te")]
            if mrow_prev is not None:
                mtw.append(mrow_prev)
            p.op("vector", lambda e: e.tensor_copy(mT16[:, :], ps_c[0:16, 0:128]),
                 waits=mtw, inc="ve")
            mtd = p.at("ve")
            mr_w = [mtd]
            if mrow_used[hp] is not None:
                mr_w.append(mrow_used[hp])
            p.op("sync", lambda e, hp=hp: e.dma_start(out=m_row[0:1, hp, :],
                                                      in_=mT16[:, :]),
                 waits=mr_w, inc="gm0" if hp == 0 else "gm1", dma=True)
            mrow_ready = p.at("gm0" if hp == 0 else "gm1")
            if debug:
                p.op("sync", lambda e, h=h: e.dma_start(
                    out=dbg["dMALL"][:, h, :], in_=m_all[:, :]),
                    waits=[mtd], inc="gdb", dma=True)
            pvc_free = mtd

            pvb = hb
            z_free = pass1_ps_free
            for qc in range(4):
                jtmax = 4 * qc + 3
                npairs = (jtmax + 1) // 2
                exp_done = None
                for pi in range(npairs):
                    jt = 2 * pi
                    sp = pi % 2
                    for dd in range(2):
                        jtc = jt + dd
                        w = [mrow_ready]
                        if psb_free is not None:
                            w.append(psb_free)
                        s = jtc - 4 * qc
                        s2 = jtc + 1 - 4 * qc
                        has_s = 0 <= s <= 3
                        has_s2 = 0 <= s2 <= 3
                        p.op("tensor", lambda e, jtc=jtc, dd=dd, qc=qc,
                             hb=hb, hc=hc: e.matmul(
                            ps_b[:, 512 * dd:512 * dd + 512],
                            kT[hb:hb + 64, hc, 128 * jtc:128 * jtc + 128],
                            qT[hb:hb + 64, hc, 512 * qc:512 * qc + 512],
                            start=True, stop=False, skip_group_check=True),
                            waits=w if dd == 0 else (), inc=None)
                        p.op("tensor", lambda e, dd=dd, qc=qc, hp=hp: e.matmul(
                            ps_b[:, 512 * dd:512 * dd + 512],
                            ones_row_bf[0:1, 0:128],
                            m_row[0:1, hp, 512 * qc:512 * qc + 512],
                            start=False, stop=(not has_s and not has_s2),
                            skip_group_check=True),
                            inc=None if (has_s or has_s2) else "te")
                        if has_s:
                            p.op("tensor", lambda e, dd=dd, s=s, hp=hp: e.matmul(
                                ps_b[:, 512 * dd + 128 * s:512 * dd + 128 * s + 128],
                                iden_bf[:, :], gcomb[:, hp, 2, :],
                                start=False, stop=(not has_s2), skip_group_check=True),
                                inc=None if has_s2 else "te")
                        if has_s2:
                            p.op("tensor", lambda e, dd=dd, s2=s2, hp=hp: e.matmul(
                                ps_b[:, 512 * dd + 128 * s2:512 * dd + 128 * s2 + 128],
                                iden_bf[:, :], gcomb[:, hp, 3, :],
                                start=False, stop=True, skip_group_check=True),
                                inc="te")
                    mmd = p.at("te")
                    pex = pexp_scr if (gpair % 2 == 0) else pexp2
                    p.op("vector", lambda e, pex=pex: e.tensor_copy(
                        pex[:, :], ps_b[:, :]),
                        waits=[mmd, exp_prev2[gpair % 2]], inc="ve")
                    psb_free = p.at("ve")
                    p.op("scalar", lambda e, sp=sp, pex=pex: e.activation(
                        ptile[:, 2 * sp:2 * sp + 2, :], pex[:, :], AF.Exp),
                        waits=[psb_free], inc="se")
                    exp_done = p.at("se")
                    exp_prev2[gpair % 2] = exp_done
                    gpair += 1
                    psb_free = exp_done
                    for dd in range(2):
                        jtc = jt + dd
                        off = 128 * max(0, jtc - 4 * qc)
                        w = [exp_done]
                        if jtc == 0 and pvc_free is not None:
                            w.append(pvc_free)
                        p.op("tensor", lambda e, jtc=jtc, dd=dd, sp=sp, off=off,
                             hc=hc: e.matmul(
                            ps_c[:, off:512],
                            v_sb[:, jtc, 128 * hc:128 * hc + 128],
                            ptile[:, 2 * sp + dd, off:512],
                            start=(jtc == 0), stop=False, skip_group_check=True),
                            waits=w, inc=None)
                        zw = []
                        if jtc == 0:
                            zw.append(z_free)
                        p.op("tensor", lambda e, dd=dd, sp=sp, off=off, jtc=jtc: e.matmul(
                            ps_a[0:1, off:512], ones_col_bf[:, :],
                            ptile[:, 2 * sp + dd, off:512],
                            start=(jtc == 0), stop=False, skip_group_check=True),
                            waits=zw, inc="te")
                pv_done = p.at("te")
                rb_w = [pv_done]
                if mrow_used[hp] is not None:
                    pass
                p.op("vector", lambda e, hp=hp: e.reciprocal(
                    r_bf[0:1, hp, 0:512], ps_a[0:1, 0:512]),
                    waits=rb_w, inc="ve")
                rd = p.at("ve")
                z_free = rd
                pass1_ps_free = rd
                p.op("tensor", lambda e, hp=hp: e.matmul(
                    ps_b[:, 0:512], ones_row_bf[0:1, 0:128],
                    r_bf[0:1, hp, 0:512], start=True, stop=True),
                    waits=[rd, psb_free] if psb_free else [rd], inc="te")
                p.op("vector", lambda e, pvb=pvb: e.tensor_copy(
                    rbc_sb[pvb:pvb + 64, :], ps_b[pvb:pvb + 64, 0:512]),
                    waits=[p.at("te")], inc="ve")
                psb_free = p.at("ve")
                p.op("vector", lambda e, pvb=pvb, hc=hc, qc=qc: e.tensor_tensor(
                    oT[pvb:pvb + 64, hc, 512 * qc:512 * qc + 512],
                    ps_c[pvb:pvb + 64, 0:512], rbc_sb[pvb:pvb + 64, :], ALU.mult),
                    inc="ve")
                pvc_free = p.at("ve")
            gc_used[hp] = p.at("te")
            mrow_used[hp] = p.at("te")
            mrow_prev = mrow_ready

        attn_done = p.at("ve")
        if debug:
            p.op("sync", lambda e: e.dma_start(out=dbg["dOT"][:, :, :], in_=oT[:, :, :]),
                 waits=[attn_done], inc="gdb", dma=True)

        p.op("sync", lambda e: e.dma_start(
            out=x_sb[:, :, :], in_=Y.rearrange("(c p) d -> p c d", p=128)),
            waits=[attn_done, x_spilled], inc="gx", dma=True)
        x_back = p.at("gx")
        def bc_build2(dst, src_row, add1, waits0):
            for c in range(2):
                p.op("tensor", lambda e, src_row=src_row, c=c: e.matmul(
                    ps_b[:, 512 * c:512 * c + 512], ones_row_f[:, :],
                    src_row[0:1, 512 * c:512 * c + 512], start=True, stop=True),
                    waits=waits0 if c == 0 else (), inc="te" if c == 1 else None)
            mmd = p.at("te")
            for c in range(2):
                if add1:
                    p.op("vector", lambda e, dst=dst, c=c: e.tensor_scalar(
                        dst[:, 512 * c:512 * c + 512], ps_b[:, 512 * c:512 * c + 512],
                        1.0, None, ALU.add), waits=[mmd], inc="ve")
                else:
                    p.op("vector", lambda e, dst=dst, c=c: e.tensor_copy(
                        dst[:, 512 * c:512 * c + 512], ps_b[:, 512 * c:512 * c + 512]),
                        waits=[mmd], inc="ve")

        bc_build2(cb1_bc, cb_row, True, [attn_done, psb_free])
        bc_build2(sb_bc, sb_row, False, [p.at("ve")])
        cbsb_ready = p.at("ve")

        evac_marks = [None] * 4
        for tt in range(16):
            for oc in range(2):
                bank = (tt * 2 + oc) % 4
                w = [attn_done, wo_ready, bc_ready, x_back]
                if evac_marks[bank] is not None:
                    w.append(evac_marks[bank])
                for it in range(8):
                    p.op("tensor", lambda e, tt=tt, oc=oc, it=it, bank=bank: e.matmul(
                        ps_a[:, 512 * bank:512 * bank + 512],
                        oT[:, it, 128 * tt:128 * tt + 128],
                        woT[:, it, 512 * oc:512 * oc + 512],
                        start=(it == 0), stop=False),
                        waits=w if it == 0 else (), inc=None)
                p.op("tensor", lambda e, oc=oc, bank=bank: e.matmul(
                    ps_a[:, 512 * bank:512 * bank + 512], ones_row_bf[0:1, :],
                    gabo_row[0:1, 512 * oc:512 * oc + 512],
                    start=False, stop=True), inc="te")
                mmd = p.at("te")
                p.op("vector", lambda e, tt=tt, oc=oc, bank=bank: e.tensor_tensor(
                    x_sb[:, tt, 512 * oc:512 * oc + 512],
                    x_sb[:, tt, 512 * oc:512 * oc + 512],
                    ps_a[:, 512 * bank:512 * bank + 512], ALU.add),
                    waits=[mmd], inc="ve")
                evac_marks[bank] = p.at("ve")
        x_attn = p.at("ve")
        p.op("sync", lambda e: e.dma_start(out=b1pp[:, :, :], in_=B1P[:, :, :]),
             waits=[p.at("te")], inc="gsm", dma=True)
        b1_ld = p.at("gsm")
        if debug:
            p.op("sync", lambda e: e.dma_start(out=dbg["dXA"][:, :, :], in_=x_sb[:, :, :]),
                 waits=[x_attn], inc="gdb", dma=True)

        ln_block(h2T, cb1_bc, sb_bc, [x_attn, cbsb_ready])
        h2_done = p.at("ve")
        if debug:
            p.op("sync", lambda e: e.dma_start(out=dbg["dH2T"][:, :, :], in_=h2T[:, :, :]),
                 waits=[h2_done], inc="gdb", dma=True)

        for tt in range(16):
            w = [h2_done, cast_small, pvc_free]
            for dt in range(8):
                p.op("tensor", lambda e, tt=tt, dt=dt: e.matmul(
                    ps_c[:, 0:NE], h2T[:, dt, 128 * tt:128 * tt + 128],
                    wg_sb[:, dt, :], start=(dt == 0), stop=(dt == 7)),
                    waits=w if dt == 0 else (), inc="te" if dt == 7 else None)
            mmd = p.at("te")
            p.op("vector", lambda e: e.reduce_max(g_m1n[:, :], ps_c[:, 0:NE], axis=AX.X,
                                                  negate=True), waits=[mmd], inc="ve")
            p.op("vector", lambda e: e.tensor_scalar(
                g_eq1[:, :], ps_c[:, 0:NE], -1.0, g_m1n[:, 0:1], ALU.mult,
                ALU.is_equal), waits=[p.at("ve")], inc="ve")
            p.op("vector", lambda e: e.tensor_scalar(
                g_msk[:, :], g_eq1[:, :], -1e9, None, ALU.mult), inc="ve")
            p.op("vector", lambda e: e.tensor_tensor(
                g_msk[:, :], g_msk[:, :], ps_c[:, 0:NE], ALU.add), inc="ve")
            gv = p.at("ve")
            p.op("vector", lambda e: e.reduce_max(g_m2n[:, :], g_msk[:, :], axis=AX.X,
                                                  negate=True), waits=[gv], inc="ve")
            pvc_free = p.at("ve")
            p.op("scalar", lambda e: e.activation(g_d[:, :], g_m2n[:, :], AF.Exp,
                                                  bias=g_m1n[:, 0:1], scale=-1.0),
                 waits=[p.at("ve")], inc="se")
            p.op("vector", lambda e: e.tensor_scalar(g_s[:, :], g_d[:, :], 1.0, None,
                                                     ALU.add), waits=[p.at("se")], inc="ve")
            p.op("vector", lambda e: e.reciprocal(g_r[:, :], g_s[:, :]),
                 waits=[p.at("ve")], inc="ve")
            p.op("vector", lambda e: e.tensor_tensor(g_w2[:, :], g_d[:, :], g_r[:, :],
                                                     ALU.mult), waits=[p.at("ve")], inc="ve")
            p.op("vector", lambda e: e.tensor_scalar(
                g_eq2[:, :], g_msk[:, :], -1.0, g_m2n[:, 0:1], ALU.mult, ALU.is_equal),
                inc="ve")
            p.op("vector", lambda e, tt=tt: e.tensor_scalar(
                coef[:, tt, :], g_eq1[:, :], g_r[:, 0:1], None, ALU.mult), inc="ve")
            p.op("vector", lambda e, tt=tt: e.scalar_tensor_tensor(
                coef[:, tt, :], g_eq2[:, :], g_w2[:, 0:1], coef[:, tt, :],
                ALU.mult, ALU.add), inc="ve")
            if debug and tt == 0:
                p.op("sync", lambda e: e.dma_start(out=dbg["dWG"][:, :, :],
                                                   in_=wg_sb[:, :, :]),
                     inc="gdb", dma=True)
                for k, src in enumerate([g_eq1, g_eq2, g_msk]):
                    p.op("vector", lambda e, k=k, src=src: e.tensor_copy(
                        scr[:, 40 * 0 + 8 * k:8 * k + 8], src[:, :]), inc="ve")
                for k, src in enumerate([g_m1n, g_m2n, g_d, g_s, g_r, g_w2]):
                    p.op("vector", lambda e, k=k, src=src: e.tensor_copy(
                        scr[:, 24 + k:25 + k], src[:, :]), inc="ve")
                p.op("sync", lambda e: e.dma_start(out=dbg["dGI"][:, 0:40],
                                                   in_=scr[:, 0:40]),
                     waits=[p.at("ve")], inc="gdb", dma=True)
        coef_done = p.at("ve")
        if debug:
            p.op("sync", lambda e: e.dma_start(out=dbg["dCOEF"][:, :, :],
                                               in_=coef[:, :, :]),
                 waits=[coef_done], inc="gdb", dma=True)

        p.op("sync", lambda e: e.dma_start(
            out=Y.rearrange("(c p) d -> p c d", p=128), in_=x_sb[:, :, :]),
            waits=[x_attn, h2_done], inc="gxs", dma=True)
        x_spill2 = p.at("gxs")

        w1_marks = [None] * 6
        m1_evac = [None] * 4
        m2_marks = [None, None, None]
        ff_marks = [None, None]
        acc_region = {}
        w2_used_te = None
        hs_used_te = None
        b2_used_te = None

        def m2slot(i):
            j = i % 3
            if j == 0:
                return ps_b[:, 0:512]
            if j == 1:
                return ps_b[:, 512:1024]
            return ps_c[:, 0:512]

        for e_i in range(NE):
            b2w = [x_spill2] if e_i == 0 else [b2_used_te]
            p.op("gpsimd", lambda e, e_i=e_i: e.dma_start(
                out=b2row[:, :], in_=B2I[e_i:e_i + 1, :]),
                waits=b2w, inc="gb2", dma=True)
            b2_ld = p.at("gb2")
            w2_chunks = []
            for fb in range(8):
                w2_chunks.append((fb,))
            w2_ld_done = None
            w2q = list(w2_chunks)

            def emit_w2_chunk():
                fb, = w2q.pop(0)
                w = []
                if e_i > 0 and w2_used_te is not None:
                    w.append(w2_used_te)
                if e_i == 0:
                    w.append(coef_done)
                p.op("gpsimd", lambda e, e_i=e_i, fb=fb: e.dma_start(
                    out=w2sb[:, 4 * fb:4 * fb + 4, :],
                    in_=W2I[e_i, 512 * fb:512 * fb + 512, :].rearrange(
                        "(c p) d -> p c d", p=128)),
                    waits=w, inc="gw2", dma=True)

            for tc in range(4):
                for ft in range(32):
                    slot = ft % 6
                    w = []
                    if w1_marks[slot] is not None:
                        w.append(w1_marks[slot])
                    if e_i == 0 and tc == 0 and ft == 0:
                        w.append(x_spill2)
                    w1sem = ["gwa", "gwb", "gwc", "gwd", "gwe", "gwf"][slot]
                    p.op("gpsimd", lambda e, e_i=e_i, ft=ft, slot=slot: e.dma_start(
                        out=w1buf[:, slot, :, :],
                        in_=W1TI[e_i, ft].rearrange("dt p f -> p dt f")),
                        waits=w, inc=w1sem, dma=True)
                    w1_ld = p.at(w1sem)
                    if tc == 0 and ft % 4 == 0 and w2q:
                        emit_w2_chunk()
                    bank = ft % 4
                    mw = [w1_ld, h2_done]
                    if m1_evac[bank] is not None:
                        mw.append(m1_evac[bank])
                    if hs_used_te is not None:
                        mw.append(hs_used_te)
                    for dt in range(8):
                        p.op("tensor", lambda e, ft=ft, dt=dt, tc=tc, slot=slot,
                             bank=bank: e.matmul(
                            ps_a[:, 512 * bank:512 * bank + 512],
                            w1buf[:, slot, dt, :],
                            h2T[:, dt, 512 * tc:512 * tc + 512],
                            start=(dt == 0), stop=(dt == 7)),
                            waits=mw if dt == 0 else (), inc="te" if dt == 7 else None)
                    w1_marks[slot] = p.at("te")
                    mmd = p.at("te")
                    p.op("vector", lambda e, ft=ft, bank=bank, e_i=e_i: e.tensor_scalar(
                        hslab[:, ft, :], ps_a[:, 512 * bank:512 * bank + 512],
                        b1pp[:, e_i, ft:ft + 1], zero_col[:, 0:1], ALU.add, ALU.max),
                        waits=[mmd, b1_ld], inc="ve")
                    m1_evac[bank] = p.at("ve")
                m1_done = p.at("ve")
                while tc == 0 and w2q:
                    emit_w2_chunk()
                if tc == 0:
                    w2_ld_done = p.at("gw2")
                for g in range(8):
                    ttl = g // 2
                    dc = g % 2
                    ttg = 4 * tc + ttl
                    si = g % 3
                    w = [m1_done, b2_ld, w2_ld_done, coef_done]
                    if m2_marks[si] is not None:
                        w.append(m2_marks[si])
                    p.op("tensor", lambda e, dc=dc, si=si: e.matmul(
                        m2slot(si), ones_row_bf[0:1, :],
                        b2row[0:1, 512 * dc:512 * dc + 512],
                        start=True, stop=False), waits=w, inc=None)
                    for ft in range(32):
                        p.op("tensor", lambda e, ft=ft, ttl=ttl, dc=dc, si=si: e.matmul(
                            m2slot(si), hslab[:, ft, 128 * ttl:128 * ttl + 128],
                            w2sb[:, ft, 512 * dc:512 * dc + 512],
                            start=False, stop=(ft == 31)),
                            inc="te" if ft == 31 else None)
                    mmd = p.at("te")
                    fw = [mmd]
                    if ff_marks[g % 2] is not None:
                        fw.append(ff_marks[g % 2])
                    p.op("vector", lambda e, g=g, ttg=ttg, e_i=e_i, si=si: e.tensor_scalar(
                        ffslot[:, g % 2, :], m2slot(si), coef[:, ttg, e_i:e_i + 1],
                        None, ALU.mult), waits=fw, inc="ve")
                    m2_marks[si] = p.at("ve")
                    gasem = "ga0" if g % 2 == 0 else "ga1"
                    accw = [m2_marks[si]]
                    if (ttg, dc) in acc_region:
                        accw.append(acc_region[(ttg, dc)])
                    p.op("gpsimd", lambda e, g=g, ttg=ttg, dc=dc, e_i=e_i: e.dma_start(
                        out=ACC[128 * ttg:128 * ttg + 128, 512 * dc:512 * dc + 512],
                        in_=ffslot[:, g % 2, :],
                        **({"accum_op": ALU.add} if e_i > 0 else {})),
                        waits=accw, inc=gasem, dma=True)
                    ff_marks[g % 2] = p.at(gasem)
                    acc_region[(ttg, dc)] = p.at(gasem)
                hs_used_te = p.at("te")
                if debug and e_i == 0 and tc == 0:
                    p.op("sync", lambda e: e.dma_start(out=dbg["dHS"][:, :, :],
                                                       in_=hslab[:, :, :]),
                         waits=[m1_done], inc="gdb", dma=True)
                    p.op("sync", lambda e: e.dma_start(out=dbg["dB1"][:, :, :],
                                                       in_=b1pp[:, :, :]),
                         inc="gdb", dma=True)
            w2_used_te = p.at("te")
            b2_used_te = p.at("te")
        acc_done0 = p.at("ga0")
        acc_done1 = p.at("ga1")

        fcw = [None, None]
        fc_ve = [None, None]
        for tt in range(16):
            sl = tt % 2
            w = [acc_done0, acc_done1, x_spill2]
            if fcw[sl] is not None:
                w.append(fcw[sl])
            if fc_ve[sl] is not None:
                w.append(fc_ve[sl])
            p.op("sync", lambda e, tt=tt, sl=sl: e.dma_start(
                out=fc_x[:, sl, :], in_=Y[128 * tt:128 * tt + 128, :]),
                waits=w, inc="gfl", dma=True)
            p.op("sync", lambda e, tt=tt, sl=sl: e.dma_start(
                out=fc_a[:, sl, :], in_=ACC[128 * tt:128 * tt + 128, :]),
                inc="gfl", dma=True)
            ld = p.at("gfl")
            p.op("vector", lambda e, sl=sl: e.tensor_tensor(
                fc_a[:, sl, :], fc_a[:, sl, :], gb_bc[:, :], ALU.mult),
                waits=[ld, bc_ready], inc="ve")
            p.op("vector", lambda e, sl=sl: e.tensor_tensor(
                fc_o[:, sl, :], fc_a[:, sl, :], fc_x[:, sl, :], ALU.add), inc="ve")
            od = p.at("ve")
            fssem = "gfs0" if sl == 0 else "gfs1"
            p.op("sync", lambda e, tt=tt, sl=sl: e.dma_start(
                out=Y[128 * tt:128 * tt + 128, :], in_=fc_o[:, sl, :]),
                waits=[od], inc=fssem, dma=True)
            fcw[sl] = p.at(fssem)
            fc_ve[sl] = od
        finals = [("gfs0", p.cnt["gfs0"]), ("gfs1", p.cnt["gfs1"])]
        if debug:
            finals.append(("gdb", p.cnt["gdb"]))
        p.op("sync", lambda e: e.nop(), waits=finals, inc=None)

        with nc.allow_low_precision(reason="bf16 operands are deliberate; accumulations stay fp32"):
            p.emit(block, limit=limit)
        build_nc.total_ops = p.order
    return nc


_NC_CACHE = {}


def _get_nc(debug=False):
    if debug not in _NC_CACHE:
        _NC_CACHE[debug] = build_nc(debug)
    return _NC_CACHE[debug]


def make_in_maps(inputs):
    x = np.asarray(inputs["L0_attn_x"])
    wq = np.ascontiguousarray(inputs["L0_attn_Wq"], dtype=np.float32)
    wk = np.ascontiguousarray(inputs["L0_attn_Wk"], dtype=np.float32)
    wv = np.ascontiguousarray(inputs["L0_attn_Wv"], dtype=np.float32)
    wo = np.ascontiguousarray(inputs["L0_attn_Wo"], dtype=np.float32)
    bq = np.asarray(inputs["L0_attn_bq"], dtype=np.float32)
    bk = np.asarray(inputs["L0_attn_bk"], dtype=np.float32)
    bv = np.asarray(inputs["L0_attn_bv"], dtype=np.float32)
    bo = np.asarray(inputs["L0_attn_bo"], dtype=np.float32)
    rel = np.asarray(inputs["L0_attn_rel_emb"], dtype=np.float32)
    wg = np.asarray(inputs["L0_ffn_Wg"], dtype=np.float32)
    w1 = np.asarray(inputs["L0_ffn_W1"], dtype=np.float32)
    b1 = np.asarray(inputs["L0_ffn_b1"], dtype=np.float32)
    w2 = np.asarray(inputs["L0_ffn_W2"], dtype=np.float32)
    b2 = np.asarray(inputs["L0_ffn_b2"], dtype=np.float32)
    c = np.asarray(inputs["L0_c"], dtype=np.float32)
    wmod = np.ascontiguousarray(inputs["L0_wmod"], dtype=np.float32)

    shared = {
        "WQ": wq, "WK": wk, "WV": wv, "WO": wo,
        "BQP": np.ascontiguousarray(bq.reshape(8, 128).T),
        "BKP": np.ascontiguousarray(bk.reshape(8, 128).T),
        "BVROW": np.ascontiguousarray(bv[None, :]),
        "BOROW": np.ascontiguousarray(bo[None, :]),
        "GB": _gather_bias_tiles(rel),
        "REL31": np.ascontiguousarray(rel[31:32, :]),
        "WGP": np.ascontiguousarray(wg.reshape(8, 128, NE).transpose(1, 0, 2)),
        "W1TI": np.ascontiguousarray(
            w1.reshape(NE, 8, 128, 32, 128).transpose(0, 3, 1, 2, 4)),
        "B1P": np.ascontiguousarray(b1.reshape(NE, 32, 128).transpose(2, 0, 1)),
        "W2I": np.ascontiguousarray(w2),
        "B2I": np.ascontiguousarray(b2),
        "WMOD": wmod,
    }
    in_maps = []
    for b in range(8):
        m = dict(shared)
        m["X"] = np.ascontiguousarray(x[b], dtype=np.float32)
        m["CP"] = np.ascontiguousarray(c[b].reshape(8, 128).T)
        in_maps.append(m)
    return in_maps


def kernel(**inputs):
    nc = _get_nc(False)
    in_maps = make_in_maps(inputs)
    res = run_bass_kernel_spmd(nc, in_maps, core_ids=list(range(8)))
    out = np.stack([np.asarray(r["Y"]) for r in res.results], axis=0)
    return out.astype(np.float32)
